# version 18
# baseline (speedup 1.0000x reference)
import numpy as np
from contextlib import ExitStack

import concourse.bass as bass
import concourse.mybir as mybir
from concourse.bass_utils import run_bass_kernel_spmd

F32 = mybir.dt.float32
BF16 = mybir.dt.bfloat16
FP16 = mybir.dt.float16
AF = mybir.ActivationFunctionType
ALU = mybir.AluOpType
AX = mybir.AxisListType

D = 1024
NT = 8
T = 2048
TH = 1024
TC = 512
II = 2816
NI = 22
HD = 256
MAGIC = 1536.0
CLIPV = 130.0
LN127 = float(np.log(127.0))
ENGS = ("sy", "ge", "ve", "se", "te")

oLNX = 0
oFT0 = 0
oFT1 = 1024
oEV0 = 2048
oEV1 = 3072
oT20 = 4096
oT21 = 5120
oWKV = 6144
oMIX = 8200
oMIXT = 16392
oKT = 17416
oRT = 25608
oVT2 = 33800
oRWK = 35848
oKH = 37896
oVH = 39944
oRH = 41992
oWP = 44040
oDEC = 47112
oRWF = 8200
oXQ = 0
oNATB = 24584
oWDQ = 17416
oPANG = 39944
oPANU = 41992
mSILO = 0
mUPS = 1024
mHCH = 2048
mPQG = 3072
mPQU = 5120
RB_ = (0, 5632)
HQ_ = (11264, 39944)
ARB_SIZE = 49920

fTMPA = 0
fZA = 0
fZB = 1025
fREC0 = 2050
fREC1 = 7170
fNUM0 = 3074
fNUM1 = 4098
fDEN0 = 5122
fDEN1 = 6146
fROWS = 3074
fROWV = 5123
fPTMP = 0
fROWG = 2816
fROWU = 5632
fROWD = 5632
fHTMP = 0
fOST = 2816
ARF_SIZE = 8448

hNMB = 0
hRSB = 2049
hRIB = 4098
hQ16 = 6147
hTMPL0 = 4098
hTMPL1 = 5123
hPT16 = 4098
hHT16 = 4098
ARH_SIZE = 8195

_CACHE = {}


def build_nc():
    nc = bass.Bass("TRN2", target_bir_lowering=False)

    xT = nc.dram_tensor("xT", [D, T + 1], F32, kind="ExternalInput").ap()
    wkT = nc.dram_tensor("wkT", [D, D], F32, kind="ExternalInput").ap()
    wvT = nc.dram_tensor("wvT", [D, D], F32, kind="ExternalInput").ap()
    wrT = nc.dram_tensor("wrT", [D, D], F32, kind="ExternalInput").ap()
    woT = nc.dram_tensor("woT", [D, D], F32, kind="ExternalInput").ap()
    wgT = nc.dram_tensor("wgT", [D, II], F32, kind="ExternalInput").ap()
    wuT = nc.dram_tensor("wuT", [D, II], F32, kind="ExternalInput").ap()
    wdT = nc.dram_tensor("wdT", [II, D], F32, kind="ExternalInput").ap()
    wgN = nc.dram_tensor("wgN", [II, D], F32, kind="ExternalInput").ap()
    wuN = nc.dram_tensor("wuN", [II, D], F32, kind="ExternalInput").ap()
    wdN = nc.dram_tensor("wdN", [D, II], F32, kind="ExternalInput").ap()
    tmki = nc.dram_tensor("tmki", [128, NT], F32, kind="ExternalInput").ap()
    tmvi = nc.dram_tensor("tmvi", [128, NT], F32, kind="ExternalInput").ap()
    tmri = nc.dram_tensor("tmri", [128, NT], F32, kind="ExternalInput").ap()
    tdec = nc.dram_tensor("tdec", [128, NT], F32, kind="ExternalInput").ap()
    tfir = nc.dram_tensor("tfir", [128, NT], F32, kind="ExternalInput").ap()
    clipx = nc.dram_tensor("clipx", [1, 1], F32, kind="ExternalInput").ap()
    clipd = nc.dram_tensor("clipd", [1, 1], F32, kind="ExternalInput").ap()
    pmask = nc.dram_tensor("pmask", [128, 8], F32, kind="ExternalInput").ap()
    outT = nc.dram_tensor("outT", [D, T], F32, kind="ExternalOutput").ap()

    h_sc = nc.dram_tensor("h_sc", [II, T], BF16, kind="Internal").ap()
    rw_sc = nc.dram_tensor("rw_sc", [D, T], BF16, kind="Internal").ap()
    cc_zin = nc.dram_tensor("cc_zin", [128, 16], F32, kind="Internal").ap()
    cc_zout = nc.dram_tensor("cc_zout", [1024, 16], F32, kind="Internal").ap()
    cc_m1i = nc.dram_tensor("cc_m1i", [1, 1], F32, kind="Internal").ap()
    cc_m1o = nc.dram_tensor("cc_m1o", [1, 1], F32, kind="Internal").ap()
    cc_m2i = nc.dram_tensor("cc_m2i", [1, 1], F32, kind="Internal").ap()
    cc_m2o = nc.dram_tensor("cc_m2o", [1, 1], F32, kind="Internal").ap()
    row_sc = nc.dram_tensor("row_sc", [1, II], F32, kind="Internal").ap()

    ctx = ExitStack()

    def sb(name, shape, dt):
        return ctx.enter_context(nc.sbuf_tensor(name, shape, dt))

    X = sb("X", [128, NT * (T + 1)], BF16)
    ARB = sb("ARB", [128, ARB_SIZE], BF16)
    MIXB = sb("MIXB", [128, NT * TH], BF16)
    ARF = sb("ARF", [128, ARF_SIZE], F32)
    ARH = sb("ARH", [128, ARH_SIZE], FP16)

    ZEA = sb("ZEA", [128, NT], F32)
    ZEB = sb("ZEB", [128, NT], F32)
    ZMSG = sb("ZMSG", [128, 16], F32)
    ZGATH = sb("ZGATH", [128, 128], F32)
    ZC = sb("ZC", [128, 16], F32)
    TMS = {"k": sb("tmk_s", [128, NT], F32), "v": sb("tmv_s", [128, NT], F32),
           "r": sb("tmr_s", [128, NT], F32)}
    OMS = {"k": sb("omk_s", [128, NT], F32), "v": sb("omv_s", [128, NT], F32),
           "r": sb("omr_s", [128, NT], F32)}
    TDS = sb("td_s", [128, NT], F32)
    UUS = sb("uu_s", [128, NT], F32)
    EUS = sb("eu_s", [128, NT], F32)
    WNEG = sb("wneg_s", [128, NT], F32)
    PHI = sb("phi_s", [128, NT], F32)
    DECF = sb("decf_s", [128, HD], F32)
    IOTA = sb("iota_s", [128, HD], F32)
    IOT1 = sb("iot1_s", [128, 128], F32)
    IOT2 = sb("iot2_s", [128, 128], F32)
    IDN = sb("idn_s", [128, 128], F32)
    ONES1B = sb("ones1b", [128, 1], BF16)
    ONES1F = sb("ones1f", [128, 1], F32)
    ONESR = sb("onesr", [1, 128], F32)
    CLXS = sb("clx_s", [1, 1], F32)
    CLDS = sb("cld_s", [1, 1], F32)
    PMS = sb("pm_s", [128, 8], F32)
    SC1 = sb("sc1_s", [1, 1], F32)
    SC2 = sb("sc2_s", [1, 1], F32)
    SC3 = sb("sc3_s", [1, 1], F32)
    SC4 = sb("sc4_s", [1, 1], F32)
    INVH = sb("invh_s", [128, 1], F32)
    SXPP = sb("sxpp_s", [128, 1], F32)
    SINVPP = sb("sinvpp_s", [128, 1], F32)
    MX8 = sb("mx8_s", [128, NT], F32)
    MXGU = sb("mxgu_s", [128, 4 * NI], F32)
    SHPP = sb("shpp_s", [128, 1], F32)
    MXACC = sb("mxacc_s", [128, 1], F32)
    MXCOL = sb("mxcol_s", [128, 1], F32)
    MAXR = sb("maxr_s", [1, 128], F32)
    SGRAW = sb("SGRAW", [128, NI], F32)
    SURAW = sb("SURAW", [128, NI], F32)
    SDRAW = sb("SDRAW", [128, NT], F32)
    SG = sb("SG", [128, NI], F32)
    SU = sb("SU", [128, NI], F32)
    SD = sb("SD", [128, NT], F32)
    TSG = sb("TSG", [32, 128], F32)
    EPS1 = sb("eps1_s", [1, 1], F32)
    L127 = sb("l127_s", [1, 1], F32)
    INVSB = sb("INVSB", [128, 2 * 128], BF16)

    PS = [ctx.enter_context(nc.psum_tensor(f"ps{j}", [128, TC], F32)) for j in range(3)]
    PST = ctx.enter_context(nc.psum_tensor("pst", [1, 3 * TC], F32))
    PBC = ctx.enter_context(nc.psum_tensor("pbc", [128, TC], F32))
    PTR = ctx.enter_context(nc.psum_tensor("ptr", [128, 256], F32))

    sem_names = ["sX", "sP0", "sP1", "sP2", "sHW", "sSR", "sWD", "sHS", "sHR0", "sHR1", "sN0", "sN1", "sN2", "sN3", "sRR0", "sRR1", "sRR2", "sRR3", "sRW0", "sRW1", "sRWR", "sSM", "sCC",
                 "sOUT", "cTE", "cVE", "cSE", "cGE",
                 "bar_sy", "bar_ge", "bar_ve", "bar_se", "bar_te"]
    sems = {n: ctx.enter_context(nc.semaphore(n)) for n in sem_names}

    def B(off, n):
        return ARB[:, off:off + n]

    def Fa(off, n):
        return ARF[:, off:off + n]

    def Ha(off, n):
        return ARH[:, off:off + n]

    def M(off, n):
        return MIXB[:, off:off + n]

    def Frow(off, n):
        return ARF[0:1, off:off + n]

    class _EngProxy:
        def __init__(self, e):
            self._e = e
            self.last = None

        def __getattr__(self, n):
            a = getattr(self._e, n)
            if callable(a):
                def w(*args, **kw):
                    r = a(*args, **kw)
                    if r is not None and hasattr(r, "then_inc"):
                        self.last = r
                    return r
                return w
            return a

    def emit(me, eng):
        eng = _EngProxy(eng)
        C = {n: 0 for n in sem_names}
        barn = [0]

        def on(name):
            return me == name

        def inc(key, n=1):
            C[key] += n
            return C[key]

        def tag(instr, key, n=1):
            if instr is not None:
                instr.then_inc(sems[key], n)
            inc(key, n)

        def wait(key, val=None):
            eng.wait_ge(sems[key], C[key] if val is None else val)

        def barrier():
            barn[0] += 1
            for n in ENGS:
                C["bar_" + n] += 1
            if me == "sy":
                eng.sem_inc(sems["bar_sy"], 1)
            else:
                eng.drain().then_inc(sems["bar_" + me], 1)
            for n in ENGS:
                if n != me:
                    eng.wait_ge(sems["bar_" + n], barn[0])

        for xc in range(4):
            lo = xc * 512
            hi = (T + 1) if xc == 3 else (xc + 1) * 512
            if on("ge"):
                eng.dma_start(out=X[:].rearrange("p (k t) -> p k t", t=T + 1)
                              [:, :, lo:hi],
                              in_=xT.rearrange("(k p) t -> p k t", p=128)[:, :, lo:hi]
                              ).then_inc(sems["sX"], 16)
            inc("sX", 16)
        for dst, src in ((TMS["k"], tmki), (TMS["v"], tmvi), (TMS["r"], tmri),
                         (TDS, tdec), (UUS, tfir), (CLXS, clipx), (CLDS, clipd),
                         (PMS, pmask)):
            if on("sy"):
                eng.dma_start(out=dst[:], in_=src).then_inc(sems["sSM"], 16)
            inc("sSM", 16)

        if on("ge"):
            eng.iota(IOTA[:], pattern=[[1, HD]], base=0, channel_multiplier=0,
                     allow_small_or_imprecise_dtypes=True)
            eng.iota(IOT1[:], pattern=[[1, 128]], base=0, channel_multiplier=0,
                     allow_small_or_imprecise_dtypes=True)
            eng.iota(IOT2[:], pattern=[[0, 128]], base=0, channel_multiplier=1,
                     allow_small_or_imprecise_dtypes=True).then_inc(sems["cGE"], 1)
        inc("cGE", 1)
        if on("ve"):
            eng.memset(EPS1[:], 1e-5)
            eng.memset(L127[:], LN127)
            eng.memset(ONES1B[:], 1.0)
            eng.memset(ONES1F[:], 1.0)
            eng.memset(ONESR[:], 1.0)
            eng.memset(MXACC[:], 0.0)
            wait("cGE")
            eng.tensor_tensor(out=IDN[:], in0=IOT1[:], in1=IOT2[:], op=ALU.is_equal)
            eng.drain().then_inc(sems["cVE"], 1)
        inc("cVE", 1)
        if on("se"):
            wait("sSM")
            eng.activation(out=WNEG[:], in_=TDS[:], func=AF.Exp)
            eng.activation(out=EUS[:], in_=UUS[:], func=AF.Exp)
            eng.drain().then_inc(sems["cSE"], 1)
        inc("cSE", 1)
        if on("ve"):
            wait("cSE")
            eng.tensor_scalar(out=WNEG[:], in0=WNEG[:], scalar1=-1.0, scalar2=None,
                              op0=ALU.mult)
            for p in ("k", "v", "r"):
                eng.tensor_scalar(out=OMS[p][:], in0=TMS[p][:], scalar1=-1.0,
                                  scalar2=1.0, op0=ALU.mult, op1=ALU.add)
            eng.drain().then_inc(sems["cVE"], 1)
        inc("cVE", 1)
        if on("se"):
            wait("cVE")
            eng.activation(out=PHI[:], in_=WNEG[:], func=AF.Exp)
            eng.drain().then_inc(sems["cSE"], 1)
        inc("cSE", 1)
        for i in range(NT):
            if on("ve"):
                wait("cSE")
                eng.tensor_scalar(out=DECF[:], in0=IOTA[:], scalar1=WNEG[:, i:i + 1],
                                  scalar2=None, op0=ALU.mult)
                tag(eng.last, "cVE")
            else:
                inc("cVE")
            if on("se"):
                wait("cVE")
                eng.activation(out=B(oDEC + i * HD, HD), in_=DECF[:], func=AF.Exp)
                tag(eng.last, "cSE")
            else:
                inc("cSE")
        barrier()

        def ln_stats(c0, W, keep=False):
            chunks = []
            o = 0
            while o < W:
                w = min(TC, W - o)
                chunks.append((o, w))
                o += w
            groups = [chunks[j:j + 3] for j in range(0, len(chunks), 3)]
            for gi1, grp in enumerate(groups):
                if on("te"):
                    wait("sX")
                    if gi1 > 0:
                        wait("cVE")
                    for ci, (o, w) in enumerate(grp):
                        for i in range(NT):
                            eng.matmul(PST[:, ci * TC:ci * TC + w], ONES1B[:],
                                       X[:, i * (T + 1) + c0 + o: i * (T + 1) + c0 + o + w],
                                       start=(i == 0), stop=(i == NT - 1))
                    tag(eng.last, "cTE")
                else:
                    inc("cTE")
                if on("ve"):
                    wait("cTE")
                    for ci, (o, w) in enumerate(grp):
                        eng.tensor_copy(out=Frow(fROWS + o, w), in_=PST[:, ci * TC:ci * TC + w])
                    tag(eng.last, "cVE")
                else:
                    inc("cVE")
            if on("ve"):
                eng.drain()
                eng.tensor_scalar(out=Frow(fROWS, W), in0=Frow(fROWS, W),
                                  scalar1=-1.0 / D, scalar2=None, op0=ALU.mult)
                tag(eng.last, "cVE")
            else:
                inc("cVE")
            bc_tags = []
            for ci, (o, w) in enumerate(chunks):
                bank = PBC if ci % 2 == 0 else PS[0]
                if on("te"):
                    if ci < 2:
                        wait("cVE")
                    else:
                        eng.wait_ge(sems["cVE"], bc_tags[ci - 2])
                    eng.matmul(bank[:, 0:w], ONESR[:], Frow(fROWS + o, w), start=True,
                               stop=True)
                    tag(eng.last, "cTE")
                else:
                    inc("cTE")
                if on("ve"):
                    wait("cTE")
                    eng.tensor_copy(out=Ha(hNMB + c0 + o, w), in_=bank[:, 0:w])
                    tag(eng.last, "cVE")
                else:
                    inc("cVE")
                bc_tags.append(C["cVE"])
            mm_tags = []
            sq_tags = []
            for i in range(NT):
                ad = oXQ + i * T if keep else oKT + (i % 2) * W
                sq = oKT + 2 * W + (i % 2) * W
                if on("ve"):
                    if i == 0:
                        wait("sX")
                        eng.drain()
                    if i >= 2 and not keep:
                        eng.wait_ge(sems["cSE"], sq_tags[i - 2])
                    eng.tensor_tensor(out=B(ad, W),
                                      in0=X[:, i * (T + 1) + c0: i * (T + 1) + c0 + W],
                                      in1=Ha(hNMB + c0, W), op=ALU.add)
                    tag(eng.last, "cVE")
                else:
                    inc("cVE")
                if on("se"):
                    wait("cVE")
                    if i >= 2:
                        eng.wait_ge(sems["cTE"], mm_tags[i - 2])
                    eng.activation(out=B(sq, W), in_=B(ad, W), func=AF.Square)
                    tag(eng.last, "cSE")
                else:
                    inc("cSE")
                sq_tags.append(C["cSE"])
                if on("te"):
                    wait("cSE")
                    if i == 0:
                        wait("cVE")
                    for ci, (o, w) in enumerate(chunks):
                        if ci < 3:
                            dst = PST[:, ci * TC:ci * TC + w]
                        elif ci == 3:
                            dst = PBC[0:1, 0:w]
                        else:
                            dst = PTR[0:1, 0:w]
                        eng.matmul(dst, ONES1B[:], B(sq + o, w),
                                   start=(i == 0), stop=(i == NT - 1))
                    tag(eng.last, "cTE")
                else:
                    inc("cTE")
                mm_tags.append(C["cTE"])
            if on("ve"):
                wait("cTE")
                for ci, (o, w) in enumerate(chunks):
                    if ci < 3:
                        src = PST[:, ci * TC:ci * TC + w]
                    elif ci == 3:
                        src = PBC[0:1, 0:w]
                    else:
                        src = PTR[0:1, 0:w]
                    eng.tensor_copy(out=Frow(fROWV + o, w), in_=src)
                tag(eng.last, "cVE")
            else:
                inc("cVE")
            if on("se"):
                wait("cVE")
                eng.activation(out=Frow(fROWV, W), in_=Frow(fROWV, W), func=AF.Ln,
                               bias=EPS1[:], scale=1.0 / D)
                eng.drain()
                eng.activation(out=Frow(fROWV, W), in_=Frow(fROWV, W), func=AF.Exp,
                               scale=-0.5)
                tag(eng.last, "cSE")
            else:
                inc("cSE")
            bc_tags = []
            for ci, (o, w) in enumerate(chunks):
                bank = PBC if ci % 2 == 0 else PS[0]
                if on("te"):
                    wait("cSE")
                    if ci < 2:
                        wait("cVE")
                    else:
                        eng.wait_ge(sems["cVE"], bc_tags[ci - 2])
                    eng.matmul(bank[:, 0:w], ONESR[:], Frow(fROWV + o, w), start=True,
                               stop=True)
                    tag(eng.last, "cTE")
                else:
                    inc("cTE")
                if on("ve"):
                    wait("cTE")
                    eng.tensor_copy(out=Ha(hRSB + c0 + o, w), in_=bank[:, 0:w])
                    tag(eng.last, "cVE")
                else:
                    inc("cVE")
                bc_tags.append(C["cVE"])
            barrier()

        ln_stats(0, T + 1)

        def nat_scales(src, ntile, width, dstcol):
            for tI in range(ntile):
                nb = oNATB + (tI % 2) * 5632
                sp = "sP%d" % (tI % 2)
                if on("ge"):
                    eng.wait_ge(sems["cVE"], max(0, C["cVE"] - 1))
                    eng.dma_start(out=B(nb, width),
                                  in_=src[tI * 128:(tI + 1) * 128, 0:width]
                                  ).then_inc(sems[sp], 16)
                inc(sp, 16)
                if on("ve"):
                    wait(sp)
                    eng.reduce_max(out=dstcol[:, tI:tI + 1], in_=B(nb, width),
                                   axis=AX.X, apply_absolute_value=True)
                    tag(eng.drain(), "cVE")
                else:
                    inc("cVE")

        def scales_to_row(scol, ncol, rowoff):
            if on("te"):
                wait("cVE")
                tag(eng.transpose(PTR[0:ncol, 0:128], scol[:], IDN[:]), "cTE")
            else:
                inc("cTE")
            if on("ve"):
                wait("cTE")
                wait("sSR")
                eng.tensor_copy(out=TSG[0:ncol, :], in_=PTR[0:ncol, 0:128])
                tag(eng.last, "cVE")
            else:
                inc("cVE")
            if on("ge"):
                wait("cVE")
                wait("sSR")
                eng.dma_start(out=row_sc[0:1, 0:ncol * 128], in_=TSG[0:ncol, :]
                              ).then_inc(sems["sSR"], 16)
            inc("sSR", 16)
            if on("ge"):
                wait("sSR")
                eng.wait_ge(sems["cVE"], C["cVE"])
                eng.dma_start(out=Frow(rowoff, ncol * 128),
                              in_=row_sc[0:1, 0:ncol * 128]).then_inc(sems["sSR"], 16)
            inc("sSR", 16)
            if on("se"):
                wait("sSR")
                eng.activation(out=Frow(rowoff, ncol * 128), in_=Frow(rowoff, ncol * 128),
                               func=AF.Ln)
                eng.drain()
                eng.activation(out=Frow(rowoff, ncol * 128), in_=Frow(rowoff, ncol * 128),
                               func=AF.Exp, scale=-1.0, bias=L127[:])
                tag(eng.last, "cSE")
            else:
                inc("cSE")


        def panel_src(srcT, e):
            return srcT.rearrange("(k p) ee -> p k ee", p=128)[:, :, e * 128:(e + 1) * 128]

        for h in range(2):
            c0 = h * TH
            half_tiles = []
            evq = []
            SLT = {}
            lnx_tags, add_tags = {}, {}
            ts_tags = {p: {} for p in ("k", "r", "v")}
            stt_tags = {p: {} for p in ("k", "r", "v")}
            mixt_reader = {}

            def panel_issue(srcw_, gt, el):
                slot = el % 3
                spn = "sP%d" % slot
                if on("ge"):
                    if slot in SLT:
                        eng.wait_ge(sems["cTE"], half_tiles[SLT[slot]])
                    eng.dma_start(out=B(oWP + slot * D, D),
                                  in_=panel_src(srcw_, el)).then_inc(sems[spn], 16)
                inc(spn, 16)
                SLT[slot] = gt

            for el in range(3):
                panel_issue(wkT, el, el)

            for i in range(NT):
                tl = hTMPL0 if i % 2 == 0 else hTMPL1
                if on("ve"):
                    if i >= 2:
                        sm2, v2 = lnx_tags[i - 2]
                        eng.wait_ge(sems[sm2], v2)
                    eng.tensor_tensor(out=Ha(tl, TH + 1),
                                      in0=X[:, i * (T + 1) + c0: i * (T + 1) + c0 + TH + 1],
                                      in1=Ha(hNMB + c0, TH + 1), op=ALU.add)
                    tag(eng.last, "cVE")
                else:
                    inc("cVE")
                add_tags[i] = C["cVE"]
                if i % 2 == 1:
                    if on("ge"):
                        eng.wait_ge(sems["cVE"], add_tags[i])
                        tag(eng.tensor_tensor(out=B(oLNX + i * (TH + 1), TH + 1),
                                              in0=Ha(tl, TH + 1),
                                              in1=Ha(hRSB + c0, TH + 1),
                                              op=ALU.mult), "cGE")
                    else:
                        inc("cGE")
                    lnx_tags[i] = ("cGE", C["cGE"])
                else:
                    if on("ve"):
                        eng.drain()
                        eng.tensor_tensor(out=B(oLNX + i * (TH + 1), TH + 1),
                                          in0=Ha(tl, TH + 1),
                                          in1=Ha(hRSB + c0, TH + 1), op=ALU.mult)
                        tag(eng.last, "cVE")
                    else:
                        inc("cVE")
                    lnx_tags[i] = ("cVE", C["cVE"])

            def mix_tile(p, i, dst):
                s_ = i % 2
                mslot = oMIXT if s_ == 0 else oDEC
                if on("se"):
                    lsem, lval = lnx_tags[i]
                    eng.wait_ge(sems[lsem], lval)
                    if s_ in mixt_reader:
                        eng.wait_ge(sems["cVE"], mixt_reader[s_])
                    eng.activation(out=B(mslot, TH), in_=B(oLNX + i * (TH + 1), TH),
                                   func=AF.Copy, scale=OMS[p][:, i:i + 1])
                    tag(eng.last, "cSE")
                else:
                    inc("cSE")
                ts_tags[p][i] = C["cSE"]
                if on("ve"):
                    eng.wait_ge(sems["cSE"], ts_tags[p][i])
                    if p == "v" and i == 0:
                        eng.wait_ge(sems["cTE"], half_tiles[7])
                    eng.scalar_tensor_tensor(out=dst[:, i * TH:(i + 1) * TH],
                                             in0=B(oLNX + i * (TH + 1) + 1, TH),
                                             scalar=TMS[p][:, i:i + 1],
                                             in1=B(mslot, TH),
                                             op0=ALU.mult, op1=ALU.add)
                    tag(eng.last, "cVE")
                else:
                    inc("cVE")
                stt_tags[p][i] = C["cVE"]
                mixt_reader[s_] = C["cVE"]

            def proj_mms(p, srcw, dsto, gbase, mixap):
                for e in range(NT):
                    if p == "r" and e == 0:
                        panel_issue(srcw, gbase, 0)
                    if e + 1 < NT and (p == "r" or e >= 2):
                        panel_issue(srcw, gbase + e + 1, e + 1)
                    sp = "sP%d" % (e % 3)
                    for c in range(2):
                        ug = gbase * 2 + e * 2 + c
                        pj = ug % 3
                        if on("te"):
                            wait(sp)
                            if ug >= 3:
                                eng.wait_ge(sems["cSE"], evq[ug - 3])
                            for ki in range(NT):
                                if e == 0 and c == 0:
                                    eng.wait_ge(sems["cVE"], stt_tags[p][ki])
                                eng.matmul(PS[pj][:],
                                           B(oWP + (e % 3) * D + ki * 128, 128),
                                           mixap[:, ki * TH + c * TC:
                                                 ki * TH + c * TC + TC],
                                           start=(ki == 0), stop=(ki == NT - 1))
                            tag(eng.last, "cTE")
                        else:
                            inc("cTE")
                        if on("se"):
                            wait("cTE")
                            o = dsto + e * TH + c * TC
                            if p == "r":
                                eng.activation(out=B(o, TC), in_=PS[pj][:],
                                               func=AF.Sigmoid)
                            else:
                                eng.activation(out=B(o, TC), in_=PS[pj][:],
                                               func=AF.Exp)
                            tag(eng.last, "cSE")
                        else:
                            inc("cSE")
                        evq.append(C["cSE"])
                    half_tiles.append(C["cTE"])

            for i in range(NT):
                mix_tile("k", i, ARB[:, oMIX:oMIX + NT * TH])
            for i in range(NT):
                mix_tile("r", i, MIXB)
            proj_mms("k", wkT, oKT, 0, ARB[:, oMIX:oMIX + NT * TH])
            for i in range(NT):
                mix_tile("v", i, ARB[:, oMIX:oMIX + NT * TH])
            proj_mms("r", wrT, oRT, 8, MIXB)

            te_chunk = {}
            evt2_tags, wk_tags, ft_tags, rc_tags, sc_tags = {}, {}, {}, {}, {}
            wk1_tags, t2s_tags = {}, {}
            oFT_ = (oFT0, oFT1)
            oEV_ = (oEV0, oEV1)
            oT2_ = (oT20, oT21)
            fNUM_ = (fNUM0, fNUM1)
            fDEN_ = (fDEN0, fDEN1)
            fREC_ = (fREC0, fREC1)

            def scan_rc(e2):
                s2 = e2 % 2
                if on("se"):
                    eng.wait_ge(sems["cVE"], sc_tags[e2])
                    if e2 >= 2:
                        eng.wait_ge(sems["cGE"], wk1_tags[e2 - 2])
                    eng.activation(out=Fa(fDEN_[s2], TH), in_=Fa(fDEN_[s2], TH),
                                   func=AF.Ln)
                    eng.drain()
                    eng.activation(out=Fa(fREC_[s2], TH), in_=Fa(fDEN_[s2], TH),
                                   func=AF.Exp, scale=-1.0)
                    tag(eng.last, "cSE")
                else:
                    inc("cSE")
                rc_tags[e2] = C["cSE"]

            def scan_wk1(e2):
                s2 = e2 % 2
                if on("ge"):
                    eng.wait_ge(sems["cSE"], rc_tags[e2])
                    if e2 >= 2:
                        eng.wait_ge(sems["cVE"], wk_tags[e2 - 2])
                    tag(eng.tensor_tensor(out=B(oWKV + s2 * TH, TH),
                                          in0=Fa(fNUM_[s2], TH),
                                          in1=Fa(fREC_[s2], TH), op=ALU.mult), "cGE")
                else:
                    inc("cGE")
                wk1_tags[e2] = C["cGE"]

            def scan_wk(e2):
                s2 = e2 % 2
                g2 = h * NT + e2
                if on("ve"):
                    eng.wait_ge(sems["cGE"], wk1_tags[e2])
                    if g2 >= 2:
                        eng.wait_ge(sems["sRW%d" % (e2 % 2)], 16 * ((g2 - 2) // 2 + 1))
                    eng.tensor_tensor(out=B(oRWK + s2 * TH, TH), in0=B(oWKV + s2 * TH, TH),
                                      in1=B(oRT + e2 * TH, TH), op=ALU.mult)
                    tag(eng.last, "cVE")
                else:
                    inc("cVE")
                wk_tags[e2] = C["cVE"]

            def scan_store(e2):
                rws = "sRW%d" % (e2 % 2)
                if on("sy"):
                    eng.wait_ge(sems["cVE"], wk_tags[e2])
                    eng.dma_start(out=rw_sc[e2 * 128:(e2 + 1) * 128,
                                            h * TH:(h + 1) * TH],
                                  in_=B(oRWK + (e2 % 2) * TH, TH)
                                  ).then_inc(sems[rws], 16)
                inc(rws, 16)

            evt2_tags_pending = {}
            for e in range(NT):
                s = e % 2
                vslot = oVT2 + s * TH
                ea = oKT + e * TH
                evt2_tags_pending[e] = C["cGE"] + 1
                if e == 0:
                    panel_issue(wvT, 16, 0)
                if e + 1 < NT:
                    panel_issue(wvT, 16 + e + 1, e + 1)
                sp = "sP%d" % (e % 3)
                for c in range(2):
                    u = e * 2 + c
                    ug = 32 + u
                    pj = ug % 3
                    if on("te"):
                        wait(sp)
                        eng.wait_ge(sems["cSE"], evq[ug - 3])
                        for ki in range(NT):
                            if e == 0 and c == 0:
                                eng.wait_ge(sems["cVE"], stt_tags["v"][ki])
                            eng.matmul(PS[pj][:],
                                       B(oWP + (e % 3) * D + ki * 128, 128),
                                       B(oMIX + ki * TH + c * TC, TC),
                                       start=(ki == 0), stop=(ki == NT - 1))
                        tag(eng.last, "cTE")
                    else:
                        inc("cTE")
                    te_chunk[u] = C["cTE"]
                    if on("se"):
                        eng.wait_ge(sems["cTE"], te_chunk[u])
                        if e >= 2 and c == 0:
                            eng.wait_ge(sems["cGE"], evt2_tags[e - 2])
                            eng.wait_ge(sems["cVE"], sc_tags[e - 2])
                        eng.copy(out=B(vslot + c * TC, TC), in_=PS[pj][:])
                        tag(eng.last, "cSE")
                    else:
                        inc("cSE")
                    evq.append(C["cSE"])
                half_tiles.append(C["cTE"])
                if on("se"):
                    if e == 0:
                        eng.wait_ge(sems["cVE"], stt_tags["v"][7])
                    eng.activation(out=B(oFT_[s], TH), in_=B(ea, TH), func=AF.Copy,
                                   scale=EUS[:, e:e + 1])
                    tag(eng.last, "cSE")
                else:
                    inc("cSE")
                ft_tags[e] = C["cSE"]
                if on("se"):
                    eng.wait_ge(sems["cGE"], evt2_tags_pending[e])
                    if e >= 2:
                        eng.wait_ge(sems["cVE"], sc_tags[e - 2])
                    eng.activation(out=B(oT2_[s], TH), in_=B(oEV_[s], TH), func=AF.Copy,
                                   scale=EUS[:, e:e + 1])
                    tag(eng.last, "cSE")
                else:
                    inc("cSE")
                t2s_tags[e] = C["cSE"]
                if e >= 2:
                    scan_rc(e - 2)
                if on("ge"):
                    if e >= 2:
                        eng.wait_ge(sems["cVE"], sc_tags[e - 2])
                        eng.wait_ge(sems["cSE"], t2s_tags[e - 2])
                    eng.wait_ge(sems["cSE"], ft_tags[e])
                    tag(eng.tensor_tensor(out=B(oEV_[s], TH), in0=B(ea, TH),
                                          in1=B(vslot, TH), op=ALU.mult), "cGE")
                else:
                    inc("cGE")
                evt2_tags[e] = C["cGE"]
                if e >= 2:
                    scan_wk1(e - 2)
                    scan_wk(e - 2)
                    scan_store(e - 2)
                if on("ve"):
                    if e >= 2:
                        eng.wait_ge(sems["cGE"], wk1_tags[e - 2])
                        eng.wait_ge(sems["cSE"], rc_tags[e - 2])
                    else:
                        eng.wait_ge(sems["cGE"], evt2_tags[e])
                        eng.wait_ge(sems["cSE"], t2s_tags[e])
                    if e >= 1:
                        eng.drain()
                    if h == 0:
                        eng.memset(Fa(fZA, 1), 0.0)
                        eng.memset(Fa(fZB, 1), 0.0)
                        ini_a, ini_b = 0.0, 0.0
                    else:
                        eng.tensor_copy(out=Fa(fZA, 1), in_=ZEA[:, e:e + 1])
                        eng.tensor_copy(out=Fa(fZB, 1), in_=ZEB[:, e:e + 1])
                        ini_a, ini_b = ZEA[:, e:e + 1], ZEB[:, e:e + 1]
                    eng.tensor_tensor_scan(out=Fa(fZA + 1, TH),
                                           data0=PHI[:, e:e + 1].broadcast_to([128, TH]),
                                           data1=B(oEV_[s], TH), initial=ini_a,
                                           op0=ALU.mult, op1=ALU.add)
                    eng.tensor_tensor_scan(out=Fa(fZB + 1, TH),
                                           data0=PHI[:, e:e + 1].broadcast_to([128, TH]),
                                           data1=B(ea, TH), initial=ini_b,
                                           op0=ALU.mult, op1=ALU.add)
                    eng.drain()
                    if h == 0:
                        eng.tensor_copy(out=ZEA[:, e:e + 1], in_=Fa(fZA + TH, 1))
                        eng.tensor_copy(out=ZEB[:, e:e + 1], in_=Fa(fZB + TH, 1))
                        eng.tensor_copy(out=B(oKH + e * HD, HD), in_=B(ea, HD))
                        eng.tensor_copy(out=B(oVH + e * HD, HD), in_=B(vslot, HD))
                        eng.tensor_copy(out=B(oRH + e * HD, HD), in_=B(oRT + e * TH, HD))
                    else:
                        eng.tensor_copy(out=ZMSG[:, e:e + 1], in_=Fa(fZA + TH, 1))
                        eng.tensor_copy(out=ZMSG[:, 8 + e:9 + e], in_=Fa(fZB + TH, 1))
                    eng.tensor_tensor(out=Fa(fNUM_[s], TH), in0=Fa(fZA, TH),
                                      in1=B(oT2_[s], TH), op=ALU.add)
                    eng.tensor_tensor(out=Fa(fDEN_[s], TH), in0=Fa(fZB, TH),
                                      in1=B(oFT_[s], TH), op=ALU.add)
                    tag(eng.last, "cVE")
                else:
                    inc("cVE")
                sc_tags[e] = C["cVE"]
            for e2 in (NT - 2, NT - 1):
                scan_rc(e2)
                scan_wk1(e2)
                scan_wk(e2)
            for e2 in (NT - 2, NT - 1):
                scan_store(e2)
            barrier()

        if on("ge"):
            eng.dma_start(out=cc_zin, in_=ZMSG[:]).then_inc(sems["sHW"], 16)
        inc("sHW", 16)
        if on("ge"):
            wait("sHW")
            eng.collective_compute("AllGather", ALU.bypass, ins=[cc_zin],
                                   outs=[cc_zout], replica_groups=[list(range(8))]
                                   ).then_inc(sems["sCC"], 1)
        inc("sCC", 1)
        for i in range(NT):
            if on("ve"):
                if i > 0:
                    wait("cSE", C["cSE"])
                eng.tensor_scalar(out=DECF[:], in0=IOTA[:], scalar1=WNEG[:, i:i + 1],
                                  scalar2=None, op0=ALU.mult)
                tag(eng.last, "cVE")
            else:
                inc("cVE")
            if on("se"):
                wait("cVE")
                eng.activation(out=B(oDEC + i * HD, HD), in_=DECF[:], func=AF.Exp)
                tag(eng.last, "cSE")
            else:
                inc("cSE")
        dec_se = C["cSE"]
        if on("sy"):
            wait("sRW0")
            wait("sRW1")
        for cr in range(4):
            if on("sy"):
                eng.dma_start(out=ARB[:, oRWF:oRWF + NT * T]
                              .rearrange("p (k t) -> p k t", t=T)
                              [:, :, cr * TC:(cr + 1) * TC],
                              in_=rw_sc.rearrange("(k p) t -> p k t", p=128)
                              [:, :, cr * TC:(cr + 1) * TC]
                              ).then_inc(sems["sRR%d" % cr], 16)
            inc("sRR%d" % cr, 16)
        PSX = PS + [PBC]
        nat_jobs = ([(wgN, tI, D, SGRAW) for tI in range(NI)]
                    + [(wuN, tI, D, SURAW) for tI in range(NI)]
                    + [(wdN, tI, II, SDRAW) for tI in range(NT)])
        nat_red_tags = []
        nat_pos = [0]

        def emit_nat(njobs):
            for _ in range(njobs):
                j = nat_pos[0]
                if j >= len(nat_jobs):
                    return
                nat_pos[0] += 1
                srcw_, tI_, width_, dstcol_ = nat_jobs[j]
                nb = oNATB + (j % 4) * 2816
                spn = "sN%d" % (j % 4)
                if on("ge"):
                    if j >= 4:
                        eng.wait_ge(sems["cVE"], nat_red_tags[j - 4])
                    eng.dma_start(out=B(nb, width_),
                                  in_=srcw_[tI_ * 128:(tI_ + 1) * 128, 0:width_]
                                  ).then_inc(sems[spn], 16)
                inc(spn, 16)
                if on("ve"):
                    wait(spn)
                    eng.reduce_max(out=dstcol_[:, tI_:tI_ + 1], in_=B(nb, width_),
                                   axis=AX.X, apply_absolute_value=True)
                    tag(eng.last, "cVE")
                else:
                    inc("cVE")
                nat_red_tags.append(C["cVE"])

        wo_te_tags = []
        wo_ev_tags = []
        wo_u = [0]

        def emit_wo_group(e, c):
            u = wo_u[0]
            wo_u[0] += 1
            pj = PSX[u % 4]
            sp = "sP%d" % (e % 3)
            if on("te"):
                wait("sRR%d" % c)
                wait(sp)
                if u >= 4:
                    eng.wait_ge(sems["cVE"], wo_ev_tags[u - 4])
                for ki in range(NT):
                    eng.matmul(pj[:], B(oWP + (e % 3) * D + ki * 128, 128),
                               B(oRWF + ki * T + c * TC, TC),
                               start=(ki == 0), stop=(ki == NT - 1))
                tag(eng.last, "cTE")
            else:
                inc("cTE")
            if on("ve"):
                wait("cTE")
                xo = e * (T + 1) + 1 + c * TC
                eng.tensor_tensor(out=X[:, xo:xo + TC], in0=X[:, xo:xo + TC],
                                  in1=pj[:], op=ALU.add)
                tag(eng.last, "cVE")
            else:
                inc("cVE")
            wo_ev_tags.append(C["cVE"])

        for e in range(NT):
            sp = "sP%d" % (e % 3)
            if on("ge"):
                if e >= 3:
                    eng.wait_ge(sems["cTE"], wo_te_tags[e - 3])
                eng.dma_start(out=B(oWP + (e % 3) * D, D),
                              in_=panel_src(woT, e)).then_inc(sems[sp], 16)
            inc(sp, 16)
            for c in (1, 2, 3):
                emit_wo_group(e, c)
            wo_te_tags.append(C["cTE"])
            emit_nat(7)
        emit_nat(len(nat_jobs))

        if on("ge"):
            wait("sCC")
            eng.dma_start(out=ZGATH[:],
                          in_=cc_zout.rearrange("(r p) f -> p r f", p=128)
                          ).then_inc(sems["sHW"], 16)
        inc("sHW", 16)
        if on("ve"):
            wait("sHW")
            eng.memset(ZC[:], 0.0)
            for r in range(8):
                eng.drain()
                eng.scalar_tensor_tensor(out=ZC[:], in0=ZGATH[:, r * 16:(r + 1) * 16],
                                         scalar=PMS[:, r:r + 1], in1=ZC[:],
                                         op0=ALU.mult, op1=ALU.add)
            tag(eng.last, "cVE")
        else:
            inc("cVE")

        zA8, zB8, fDENa, fRECa = 0, 2056, 4112, 6160
        bFT, bEV, bT2, bNUM = 0, 2048, 4096, 6144
        bWKV = 2048
        za_v = ARF[:, zA8:zA8 + 2056].rearrange("p (i t) -> p i t", t=257)[:, :, 0:256]
        zb_v = ARF[:, zB8:zB8 + 2056].rearrange("p (i t) -> p i t", t=257)[:, :, 0:256]
        t2_v = ARB[:, bT2:bT2 + 2048].rearrange("p (i t) -> p i t", t=256)
        ft_v = ARB[:, bFT:bFT + 2048].rearrange("p (i t) -> p i t", t=256)
        num_v = ARB[:, bNUM:bNUM + 2048].rearrange("p (i t) -> p i t", t=256)
        den_v = ARF[:, fDENa:fDENa + 2048].rearrange("p (i t) -> p i t", t=256)
        if on("se"):
            for i in range(NT):
                eng.activation(out=B(bFT + i * HD, HD), in_=B(oKH + i * HD, HD),
                               func=AF.Copy, scale=EUS[:, i:i + 1])
            tag(eng.last, "cSE")
        else:
            inc("cSE")
        if on("ve"):
            eng.tensor_tensor(out=B(bEV, 2048), in0=B(oKH, 2048), in1=B(oVH, 2048),
                              op=ALU.mult)
            for i in range(NT):
                eng.memset(Fa(zA8 + i * 257, 1), 0.0)
                eng.memset(Fa(zB8 + i * 257, 1), 0.0)
            eng.drain()
            for i in range(NT):
                eng.tensor_tensor_scan(out=Fa(zA8 + i * 257 + 1, HD),
                                       data0=PHI[:, i:i + 1].broadcast_to([128, HD]),
                                       data1=B(bEV + i * HD, HD), initial=0.0,
                                       op0=ALU.mult, op1=ALU.add)
                eng.tensor_tensor_scan(out=Fa(zB8 + i * 257 + 1, HD),
                                       data0=PHI[:, i:i + 1].broadcast_to([128, HD]),
                                       data1=B(oKH + i * HD, HD), initial=0.0,
                                       op0=ALU.mult, op1=ALU.add)
            eng.drain()
            eng.wait_ge(sems["cSE"], dec_se)
            for i in range(NT):
                eng.scalar_tensor_tensor(out=Fa(zA8 + i * 257, HD),
                                         in0=B(oDEC + i * HD, HD),
                                         scalar=ZC[:, i:i + 1],
                                         in1=Fa(zA8 + i * 257, HD),
                                         op0=ALU.mult, op1=ALU.add)
                eng.scalar_tensor_tensor(out=Fa(zB8 + i * 257, HD),
                                         in0=B(oDEC + i * HD, HD),
                                         scalar=ZC[:, 8 + i:9 + i],
                                         in1=Fa(zB8 + i * 257, HD),
                                         op0=ALU.mult, op1=ALU.add)
            wait("cSE")
            eng.tensor_tensor(out=B(bT2, 2048), in0=B(bFT, 2048), in1=B(oVH, 2048),
                              op=ALU.mult)
            eng.drain()
            eng.tensor_tensor(out=num_v, in0=za_v, in1=t2_v, op=ALU.add)
            eng.tensor_tensor(out=den_v, in0=zb_v, in1=ft_v, op=ALU.add)
            tag(eng.last, "cVE")
        else:
            inc("cVE")
        if on("se"):
            wait("cVE")
            eng.activation(out=Fa(fDENa, 2048), in_=Fa(fDENa, 2048), func=AF.Ln)
            eng.drain()
            eng.activation(out=Fa(fRECa, 2048), in_=Fa(fDENa, 2048), func=AF.Exp,
                           scale=-1.0)
            tag(eng.last, "cSE")
        else:
            inc("cSE")
        if on("ve"):
            wait("cSE")
            wait("sRR0")
            eng.tensor_tensor(out=B(bWKV, 2048), in0=B(bNUM, 2048),
                              in1=Fa(fRECa, 2048), op=ALU.mult)
            eng.drain()
            eng.tensor_tensor(out=ARB[:, oRWF:oRWF + NT * T]
                              .rearrange("p (i t) -> p i t", t=T)[:, :, 0:HD],
                              in0=ARB[:, bWKV:bWKV + 2048]
                              .rearrange("p (i t) -> p i t", t=256),
                              in1=ARB[:, oRH:oRH + 2048]
                              .rearrange("p (i t) -> p i t", t=256), op=ALU.mult)
            tag(eng.last, "cVE")
        else:
            inc("cVE")
        fixup_tag = C["cVE"]

        if on("te"):
            eng.wait_ge(sems["cVE"], fixup_tag)
        for e in range(NT):
            sp = "sP%d" % (e % 3)
            if on("ge"):
                lt = {0: 6, 1: 7, 2: 5}[e] if e < 3 else NT + e - 3
                eng.wait_ge(sems["cTE"], wo_te_tags[lt])
                eng.dma_start(out=B(oWP + (e % 3) * D, D),
                              in_=panel_src(woT, e)).then_inc(sems[sp], 16)
            inc(sp, 16)
            emit_wo_group(e, 0)
            wo_te_tags.append(C["cTE"])
        barrier()

        ln_stats(1, T, keep=True)

        for i in range(NT):
            if on("ve"):
                if i >= 1:
                    eng.drain()
                eng.tensor_tensor(out=Fa(fTMPA, T),
                                  in0=B(oXQ + i * T, T),
                                  in1=Ha(hRSB + 1, T), op=ALU.mult)
                eng.drain()
                eng.reduce_max(out=MX8[:, i:i + 1], in_=Fa(fTMPA, T), axis=AX.X,
                               apply_absolute_value=True)
                tag(eng.last, "cVE")
            else:
                inc("cVE")
        if on("ve"):
            eng.drain()
            eng.reduce_max(out=MXACC[:], in_=MX8[:], axis=AX.X,
                           apply_absolute_value=True)
            tag(eng.last, "cVE")
        else:
            inc("cVE")
        if on("te"):
            wait("cVE")
            tag(eng.transpose(PTR[0:1, 0:128], MXACC[:], IDN[:]), "cTE")
        else:
            inc("cTE")
        if on("ve"):
            wait("cTE")
            eng.tensor_copy(out=MAXR[:], in_=PTR[0:1, 0:128])
            eng.drain()
            eng.reduce_max(out=SC1[:], in_=MAXR[:], axis=AX.X, apply_absolute_value=True)
            tag(eng.last, "cVE")
        else:
            inc("cVE")
        if on("ge"):
            wait("cVE")
            eng.dma_start(out=cc_m1i, in_=SC1[:]).then_inc(sems["sHW"], 16)
        inc("sHW", 16)
        if on("ge"):
            wait("sHW")
            eng.collective_compute("AllReduce", ALU.max, ins=[cc_m1i], outs=[cc_m1o],
                                   replica_groups=[list(range(8))]
                                   ).then_inc(sems["sCC"], 1)
        inc("sCC", 1)

        scales_to_row(SDRAW, NT, fROWD)
        wdq_dst = ARB[:, oWDQ:oWDQ + NI * D].rearrange("p (k hh) -> p k hh", hh=D)
        wdq_g = {}
        for e in range(NT):
            if on("ge"):
                eng.wait_ge(sems["cVE"], C["cVE"])
                eng.dma_start(out=wdq_dst[:, :, e * 128:(e + 1) * 128],
                              in_=wdT.rearrange("(k p) hh -> p k hh", p=128)
                              [:, :, e * 128:(e + 1) * 128]).then_inc(sems["sWD"], 16)
            inc("sWD", 16)
            if on("te"):
                if e == 0:
                    wait("cSE")
                else:
                    wait("cVE")
                eng.matmul(PBC[:, 0:128], ONESR[:], Frow(fROWD + e * 128, 128),
                           start=True, stop=True)
                tag(eng.last, "cTE")
            else:
                inc("cTE")
            iv = (e % 2) * 128
            pt = fPTMP + (e % 2) * (NI * 128)
            hq = 4098
            if on("ve"):
                wait("cTE")
                if e >= 2:
                    eng.wait_ge(sems["cGE"], wdq_g[e - 2])
                eng.tensor_copy(out=INVSB[:, iv:iv + 128], in_=PBC[:, 0:128])
                tag(eng.last, "cVE")
            else:
                inc("cVE")
            wdq_iv = C["cVE"]
            if on("ge"):
                eng.wait_ge(sems["cVE"], wdq_iv)
                wait("sWD")
                tag(eng.tensor_tensor(
                    out=Fa(pt, NI * 128).rearrange("p (k f) -> p k f", f=128),
                    in0=wdq_dst[:, :, e * 128:(e + 1) * 128],
                    in1=INVSB[:, iv:iv + 128].rearrange("p (a f) -> p a f", a=1)
                    .broadcast_to([128, NI, 128]), op=ALU.mult), "cGE")
            else:
                inc("cGE")
            wdq_g[e] = C["cGE"]
            if on("ve"):
                wait("cGE")
                if e > 0:
                    eng.drain()
                eng.tensor_scalar(out=Ha(hq, NI * 128), in0=Fa(pt, NI * 128),
                                  scalar1=MAGIC, scalar2=None, op0=ALU.add)
                eng.drain()
                eng.tensor_scalar(out=wdq_dst[:, :, e * 128:(e + 1) * 128],
                                  in0=Ha(hq, NI * 128),
                                  scalar1=MAGIC, scalar2=None, op0=ALU.subtract)
                tag(eng.last, "cVE")
            else:
                inc("cVE")

        PSX = PS + [PBC]
        gu_load_tags = {}
        gu_bc_tags = {}
        gu_q_tags = {}
        gu_mm_tags = {}
        gu_se_tags = []
        gu_hv_tags = []
        gu_hg_tags = []

        def gu_load(tJ):
            spn = "sP%d" % (tJ % 2)
            pb = (tJ % 2) * 1024
            if on("ge"):
                if tJ >= 2:
                    eng.wait_ge(sems["cVE"], gu_q_tags[tJ - 2])
                eng.dma_start(out=B(oPANG + pb, 1024), in_=panel_src(wgT, tJ)
                              ).then_inc(sems[spn], 16)
                eng.dma_start(out=B(oPANU + pb, 1024), in_=panel_src(wuT, tJ)
                              ).then_inc(sems[spn], 16)
            inc(spn, 32)
            gu_load_tags[tJ] = (spn, C[spn])

        def gu_bcast(tJ):
            if on("te"):
                if tJ < 2:
                    wait("cSE")
                    wait("cVE")
                if tJ >= 1:
                    eng.wait_ge(sems["cVE"], gu_q_tags[tJ - 1])
                eng.matmul(PTR[:, 0:128], ONESR[:],
                           ARF[0:1, fROWG + tJ * 128: fROWG + tJ * 128 + 128],
                           start=True, stop=True)
                eng.matmul(PTR[:, 128:256], ONESR[:],
                           ARF[0:1, fROWU + tJ * 128: fROWU + tJ * 128 + 128],
                           start=True, stop=True)
                tag(eng.last, "cTE")
            else:
                inc("cTE")
            gu_bc_tags[tJ] = C["cTE"]

        def gu_quant(tJ):
            pb = (tJ % 2) * 1024
            spn, spv = gu_load_tags[tJ]
            if on("ve"):
                eng.wait_ge(sems["cTE"], gu_bc_tags[tJ])
                eng.wait_ge(sems[spn], spv)
                if tJ == 0:
                    eng.wait_ge(sems["cGE"], C["cGE"])
                if tJ >= 2:
                    eng.wait_ge(sems["cTE"], gu_mm_tags[tJ - 2])
                eng.tensor_copy(out=INVSB[:, 0:256], in_=PTR[:, 0:256])
                for which, po, qo in ((0, oPANG, mPQG), (1, oPANU, mPQU)):
                    eng.drain()
                    eng.tensor_tensor(out=Fa(fPTMP, 1024).rearrange("p (k f) -> p k f", f=128),
                                      in0=B(po + pb, 1024).rearrange("p (k f) -> p k f", f=128),
                                      in1=INVSB[:, which * 128:(which + 1) * 128]
                                      .rearrange("p (a f) -> p a f", a=1)
                                      .broadcast_to([128, NT, 128]), op=ALU.mult)
                    eng.drain()
                    eng.tensor_scalar(out=Ha(hPT16, 1024), in0=Fa(fPTMP, 1024),
                                      scalar1=MAGIC, scalar2=None, op0=ALU.add)
                    eng.drain()
                    eng.tensor_scalar(out=M(qo + pb, 1024), in0=Ha(hPT16, 1024),
                                      scalar1=MAGIC, scalar2=None, op0=ALU.subtract)
                tag(eng.last, "cVE")
            else:
                inc("cVE")
            gu_q_tags[tJ] = C["cVE"]

        scales_to_row(SGRAW, NI, fROWG)
        scales_to_row(SURAW, NI, fROWU)
        gu_load(0)
        gu_bcast(0)
        gu_quant(0)
        gu_load(1)
        gu_bcast(1)
        gu_quant(1)

        if on("ge"):
            wait("sCC")
            eng.dma_start(out=SC2[:], in_=cc_m1o).then_inc(sems["sHW"], 16)
        inc("sHW", 16)
        if on("se"):
            wait("sHW")
            eng.activation(out=SC3[:], in_=CLXS[:], func=AF.Sigmoid)
            tag(eng.last, "cSE")
        else:
            inc("cSE")
        if on("ve"):
            wait("cSE")
            eng.tensor_tensor(out=SC3[:], in0=SC3[:], in1=SC2[:], op=ALU.mult)
            eng.drain()
            eng.tensor_scalar(out=SC3[:], in0=SC3[:], scalar1=1.0 / 127.0, scalar2=None,
                              op0=ALU.mult)
            tag(eng.last, "cVE")
        else:
            inc("cVE")
        if on("se"):
            wait("cVE")
            eng.activation(out=SC4[:], in_=SC3[:], func=AF.Ln)
            eng.drain()
            eng.activation(out=SC4[:], in_=SC4[:], func=AF.Exp, scale=-1.0)
            tag(eng.last, "cSE")
        else:
            inc("cSE")
        if on("te"):
            wait("cSE")
            wait("cVE")
            eng.matmul(PBC[:, 0:1], ONESR[:], SC3[:], start=True, stop=True)
            eng.matmul(PBC[:, 1:2], ONESR[:], SC4[:], start=True, stop=True)
            tag(eng.last, "cTE")
        else:
            inc("cTE")
        if on("ve"):
            wait("cTE")
            eng.tensor_copy(out=SXPP[:], in_=PBC[:, 0:1])
            eng.tensor_copy(out=SINVPP[:], in_=PBC[:, 1:2])
            eng.drain()
            eng.tensor_scalar(out=SG[:], in0=SGRAW[:], scalar1=SXPP[:],
                              scalar2=1.0 / 127.0, op0=ALU.mult, op1=ALU.mult)
            eng.tensor_scalar(out=SU[:], in0=SURAW[:], scalar1=SXPP[:],
                              scalar2=1.0 / 127.0, op0=ALU.mult, op1=ALU.mult)
            tag(eng.last, "cVE")
        else:
            inc("cVE")
        barrier()
        xq_col = {}
        for c4 in range(4):
            co = c4 * TC
            if on("ve"):
                for g4 in range(2):
                    eng.drain()
                    for i in range(g4 * 4, g4 * 4 + 4):
                        eng.tensor_tensor(out=Fa(fTMPA + (i % 4) * TC, TC),
                                          in0=B(oXQ + i * T + co, TC),
                                          in1=Ha(hRSB + 1 + co, TC), op=ALU.mult)
                    eng.drain()
                    for i in range(g4 * 4, g4 * 4 + 4):
                        eng.tensor_scalar(out=Ha(hQ16 + (i % 4) * TC, TC),
                                          in0=Fa(fTMPA + (i % 4) * TC, TC),
                                          scalar1=SINVPP[:], scalar2=MAGIC,
                                          op0=ALU.mult, op1=ALU.add)
                    eng.drain()
                    for i in range(g4 * 4, g4 * 4 + 4):
                        eng.tensor_scalar(out=Ha(hQ16 + (i % 4) * TC, TC),
                                          in0=Ha(hQ16 + (i % 4) * TC, TC),
                                          scalar1=MAGIC + 127.0, scalar2=MAGIC - 128.0,
                                          op0=ALU.min, op1=ALU.max)
                    eng.drain()
                    for i in range(g4 * 4, g4 * 4 + 4):
                        eng.tensor_scalar(out=B(oXQ + i * T + co, TC),
                                          in0=Ha(hQ16 + (i % 4) * TC, TC),
                                          scalar1=MAGIC, scalar2=None, op0=ALU.subtract)
                tag(eng.last, "cVE")
            else:
                inc("cVE")
            xq_col[c4] = C["cVE"]
        for tI in range(NI):
            pb = (tI % 2) * 1024
            if 2 <= tI + 1 < NI:
                gu_load(tI + 1)
                gu_bcast(tI + 1)
            for c in range(4):
                if c == 2 and 2 <= tI + 1 < NI:
                    gu_quant(tI + 1)
                m = tI * 4 + c
                bg = PSX[(2 * m) % 4]
                bu = PSX[(2 * m + 1) % 4]
                if on("te"):
                    if tI == 0:
                        eng.wait_ge(sems["cVE"], xq_col[c])
                    if c == 0:
                        eng.wait_ge(sems["cVE"], gu_q_tags[tI])
                    if m >= 2:
                        eng.wait_ge(sems["cSE"], gu_se_tags[m - 2])
                    for ki in range(NT):
                        eng.matmul(bg[:], M(mPQG + pb + ki * 128, 128),
                                   B(oXQ + ki * T + c * TC, TC),
                                   start=(ki == 0), stop=(ki == NT - 1))
                    tag(eng.last, "cTE")
                    for ki in range(NT):
                        eng.matmul(bu[:], M(mPQU + pb + ki * 128, 128),
                                   B(oXQ + ki * T + c * TC, TC),
                                   start=(ki == 0), stop=(ki == NT - 1))
                    tag(eng.last, "cTE")
                else:
                    inc("cTE", 2)
                gu_mm_tags[tI] = C["cTE"]
                so = mSILO + (m % 2) * TC
                uo = mUPS + (m % 2) * TC
                if on("se"):
                    eng.wait_ge(sems["cTE"], C["cTE"] - 1)
                    if m >= 2:
                        eng.wait_ge(sems["cGE"], gu_hg_tags[m - 2])
                    eng.activation(out=M(so, TC), in_=bg[:], func=AF.Silu,
                                   scale=SG[:, tI:tI + 1])
                    wait("cTE")
                    eng.activation(out=M(uo, TC), in_=bu[:], func=AF.Copy,
                                   scale=SU[:, tI:tI + 1])
                    tag(eng.last, "cSE")
                else:
                    inc("cSE")
                gu_se_tags.append(C["cSE"])
                hb = mHCH + (m % 2) * TC
                if on("ge"):
                    wait("cSE")
                    if m >= 2:
                        wait("sHS")
                        eng.wait_ge(sems["cVE"], gu_hv_tags[m - 2])
                    tag(eng.tensor_tensor(out=M(hb, TC), in0=M(so, TC), in1=M(uo, TC),
                                          op=ALU.mult), "cGE")
                else:
                    inc("cGE")
                gu_hg_tags.append(C["cGE"])
                if on("ve"):
                    wait("cGE")
                    eng.reduce_max(out=MXGU[:, m:m + 1], in_=M(hb, TC), axis=AX.X,
                                   apply_absolute_value=True)
                    tag(eng.last, "cVE")
                else:
                    inc("cVE")
                gu_hv_tags.append(C["cVE"])
                if on("sy"):
                    wait("cGE")
                    eng.dma_start(out=h_sc[tI * 128:(tI + 1) * 128, c * TC:(c + 1) * TC],
                                  in_=M(hb, TC)).then_inc(sems["sHS"], 16)
                inc("sHS", 16)
        barrier()

        if on("ve"):
            eng.reduce_max(out=MXACC[:], in_=MXGU[:], axis=AX.X,
                           apply_absolute_value=True)
            tag(eng.last, "cVE")
        else:
            inc("cVE")
        if on("te"):
            wait("cVE")
            tag(eng.transpose(PTR[0:1, 0:128], MXACC[:], IDN[:]), "cTE")
        else:
            inc("cTE")
        if on("ve"):
            wait("cTE")
            eng.tensor_copy(out=MAXR[:], in_=PTR[0:1, 0:128])
            eng.drain()
            eng.reduce_max(out=SC1[:], in_=MAXR[:], axis=AX.X, apply_absolute_value=True)
            tag(eng.last, "cVE")
        else:
            inc("cVE")
        if on("ge"):
            wait("cVE")
            eng.dma_start(out=cc_m2i, in_=SC1[:]).then_inc(sems["sHW"], 16)
        inc("sHW", 16)
        if on("ge"):
            wait("sHW")
            eng.collective_compute("AllReduce", ALU.max, ins=[cc_m2i], outs=[cc_m2o],
                                   replica_groups=[list(range(8))]
                                   ).then_inc(sems["sCC"], 1)
        inc("sCC", 1)
        if on("ge"):
            wait("sCC")
            eng.dma_start(out=SC2[:], in_=cc_m2o).then_inc(sems["sHW"], 16)
        inc("sHW", 16)
        if on("se"):
            wait("sHW")
            eng.activation(out=SC3[:], in_=CLDS[:], func=AF.Sigmoid)
            tag(eng.last, "cSE")
        else:
            inc("cSE")
        if on("ve"):
            wait("cSE")
            eng.tensor_tensor(out=SC3[:], in0=SC3[:], in1=SC2[:], op=ALU.mult)
            eng.drain()
            eng.tensor_scalar(out=SC3[:], in0=SC3[:], scalar1=1.0 / 127.0, scalar2=None,
                              op0=ALU.mult)
            tag(eng.last, "cVE")
        else:
            inc("cVE")
        if on("se"):
            wait("cVE")
            eng.activation(out=SC4[:], in_=SC3[:], func=AF.Ln)
            eng.drain()
            eng.activation(out=SC4[:], in_=SC4[:], func=AF.Exp, scale=-1.0)
            tag(eng.last, "cSE")
        else:
            inc("cSE")
        if on("te"):
            wait("cSE")
            eng.matmul(PBC[:, 0:1], ONESR[:], SC3[:], start=True, stop=True)
            eng.matmul(PBC[:, 1:2], ONESR[:], SC4[:], start=True, stop=True)
            tag(eng.last, "cTE")
        else:
            inc("cTE")
        if on("ve"):
            wait("cTE")
            eng.tensor_copy(out=SHPP[:], in_=PBC[:, 0:1])
            eng.tensor_copy(out=INVH[:], in_=PBC[:, 1:2])
            eng.drain()
            eng.tensor_scalar(out=SD[:], in0=SDRAW[:], scalar1=SHPP[:],
                              scalar2=1.0 / 127.0, op0=ALU.mult, op1=ALU.mult)
            tag(eng.last, "cVE")
        else:
            inc("cVE")
        barrier()

        TC2 = 256
        NC2 = T // TC2
        HRW = NI * TC2
        dn_q_tags = {}
        dn_mm_tags = {}
        dn_ev_tags = []
        dn_u = [0]

        def dn_read(cJ):
            sl = RB_[cJ % 2]
            spn = "sHR%d" % (cJ % 2)
            if on("sy"):
                if cJ == 0:
                    wait("sHS")
                if cJ >= 2:
                    eng.wait_ge(sems["cVE"], dn_q_tags[cJ - 2])
                half1 = 11
                eng.dma_start(out=B(sl, half1 * TC2),
                              in_=h_sc.rearrange("(k p) (cc t) -> p k cc t", p=128, cc=NC2)
                              [:, 0:half1, cJ, :]).then_inc(sems[spn], 16)
                eng.dma_start(out=B(sl + half1 * TC2, (NI - half1) * TC2),
                              in_=h_sc.rearrange("(k p) (cc t) -> p k cc t", p=128, cc=NC2)
                              [:, half1:NI, cJ, :]).then_inc(sems[spn], 16)
            inc(spn, 32)

        def dn_quant(cJ):
            sl = RB_[cJ % 2]
            ql = HQ_[cJ % 2]
            spn = "sHR%d" % (cJ % 2)
            if on("ve"):
                wait(spn)
                if cJ >= 2:
                    eng.wait_ge(sems["cTE"], dn_mm_tags[cJ - 2])
                for hh in range(2):
                    qo = hh * 2816
                    eng.drain()
                    eng.tensor_scalar(out=Ha(hHT16, 2816), in0=B(sl + qo, 2816),
                                      scalar1=INVH[:], scalar2=MAGIC, op0=ALU.mult,
                                      op1=ALU.add)
                    eng.drain()
                    eng.tensor_scalar(out=Ha(hHT16, 2816), in0=Ha(hHT16, 2816),
                                      scalar1=MAGIC + 127.0, scalar2=MAGIC - 128.0,
                                      op0=ALU.min, op1=ALU.max)
                    eng.drain()
                    eng.tensor_scalar(out=B(ql + qo, 2816), in0=Ha(hHT16, 2816),
                                      scalar1=MAGIC, scalar2=None, op0=ALU.subtract)
                tag(eng.last, "cVE")
            else:
                inc("cVE")
            dn_q_tags[cJ] = C["cVE"]

        dn_read(0)
        dn_quant(0)
        for c in range(NC2):
            ql = HQ_[c % 2]
            if c + 1 < NC2:
                dn_read(c + 1)
            for e in range(NT):
                if e == 4 and c + 1 < NC2:
                    dn_quant(c + 1)
                u = dn_u[0]
                dn_u[0] += 1
                pj = PSX[u % 4]
                if on("te"):
                    if e == 0:
                        eng.wait_ge(sems["cVE"], dn_q_tags[c])
                    if u >= 4:
                        eng.wait_ge(sems["cVE"], dn_ev_tags[u - 4])
                    for ki in range(NI):
                        eng.matmul(pj[:, 0:TC2],
                                   ARB[:, oWDQ + ki * D + e * 128: oWDQ + ki * D + (e + 1) * 128],
                                   B(ql + ki * TC2, TC2),
                                   start=(ki == 0), stop=(ki == NI - 1))
                    tag(eng.last, "cTE")
                else:
                    inc("cTE")
                dn_mm_tags[c] = C["cTE"]
                ob = fOST + (u % 2) * TC
                if on("ve"):
                    wait("cTE")
                    if u >= 2:
                        wait("sOUT")
                    xo = e * (T + 1) + 1 + c * TC2
                    eng.scalar_tensor_tensor(out=Fa(ob, TC2), in0=pj[:, 0:TC2],
                                             scalar=SD[:, e:e + 1], in1=X[:, xo:xo + TC2],
                                             op0=ALU.mult, op1=ALU.add)
                    tag(eng.last, "cVE")
                else:
                    inc("cVE")
                dn_ev_tags.append(C["cVE"])
                if on("sy"):
                    wait("cVE")
                    eng.dma_start(out=outT[e * 128:(e + 1) * 128, c * TC2:(c + 1) * TC2],
                                  in_=Fa(ob, TC2)).then_inc(sems["sOUT"], 16)
                inc("sOUT", 16)
        if on("sy"):
            wait("sOUT")
        barrier()
        return C

    with nc.Block() as block:
        @block.sync
        def _(e):
            emit("sy", e)

        @block.gpsimd
        def _(e):
            emit("ge", e)

        @block.vector
        def _(e):
            emit("ve", e)

        @block.scalar
        def _(e):
            emit("se", e)

        @block.tensor
        def _(e):
            emit("te", e)

    ctx.close()
    return nc


def kernel(L0_attn_x, L0_attn_time_mix_k, L0_attn_time_mix_v, L0_attn_time_mix_r,
           L0_attn_time_decay, L0_attn_time_first, L0_attn_Wk, L0_attn_Wv,
           L0_attn_Wr, L0_attn_Wo, L0_ffn_clip_x, L0_ffn_clip_down,
           L0_ffn_w_gate, L0_ffn_w_up, L0_ffn_w_down):
    x = np.ascontiguousarray(np.asarray(L0_attn_x, dtype=np.float32))
    f32 = lambda a: np.ascontiguousarray(np.asarray(a, dtype=np.float32))
    tT = lambda a: np.ascontiguousarray(np.asarray(a, dtype=np.float32).T)

    if "nc" not in _CACHE:
        _CACHE["nc"] = build_nc()
    nc = _CACHE["nc"]

    wkT_, wvT_, wrT_, woT_ = tT(L0_attn_Wk), tT(L0_attn_Wv), tT(L0_attn_Wr), tT(L0_attn_Wo)
    wgT_, wuT_, wdT_ = tT(L0_ffn_w_gate), tT(L0_ffn_w_up), tT(L0_ffn_w_down)
    wgN_, wuN_, wdN_ = f32(L0_ffn_w_gate), f32(L0_ffn_w_up), f32(L0_ffn_w_down)

    def chan(a):
        return np.ascontiguousarray(np.asarray(a, dtype=np.float32).reshape(NT, 128).T)

    tmk_, tmv_, tmr_ = (chan(L0_attn_time_mix_k), chan(L0_attn_time_mix_v),
                        chan(L0_attn_time_mix_r))
    td_, tf_ = chan(L0_attn_time_decay), chan(L0_attn_time_first)
    clx_ = f32(L0_ffn_clip_x).reshape(1, 1)
    cld_ = f32(L0_ffn_clip_down).reshape(1, 1)

    in_maps = []
    for c in range(8):
        b, q = divmod(c, 4)
        xs = np.zeros((D, T + 1), np.float32)
        s0 = q * T
        xs[:, 1:] = x[b, s0:s0 + T].T
        if q > 0:
            xs[:, 0] = x[b, s0 - 1]
        pm = np.zeros((128, 8), np.float32)
        if q > 0:
            pm[:, c - 1] = 1.0
        in_maps.append({
            "xT": np.ascontiguousarray(xs), "wkT": wkT_, "wvT": wvT_, "wrT": wrT_,
            "woT": woT_, "wgT": wgT_, "wuT": wuT_, "wdT": wdT_,
            "wgN": wgN_, "wuN": wuN_, "wdN": wdN_,
            "tmki": tmk_, "tmvi": tmv_, "tmri": tmr_, "tdec": td_, "tfir": tf_,
            "clipx": clx_, "clipd": cld_, "pmask": pm,
        })

    res = run_bass_kernel_spmd(nc, in_maps, core_ids=list(range(8)))
    out = np.empty((2, 8192, D), np.float32)
    for c in range(8):
        b, q = divmod(c, 4)
        out[b, q * T:(q + 1) * T] = res.results[c]["outT"].T
    return out


# version 19
# speedup vs baseline: 1.0105x; 1.0105x over previous
import numpy as np
from contextlib import ExitStack

import concourse.bass as bass
import concourse.mybir as mybir
from concourse.bass_utils import run_bass_kernel_spmd

F32 = mybir.dt.float32
BF16 = mybir.dt.bfloat16
FP16 = mybir.dt.float16
AF = mybir.ActivationFunctionType
ALU = mybir.AluOpType
AX = mybir.AxisListType

D = 1024
NT = 8
T = 2048
TH = 1024
TC = 512
II = 2816
NI = 22
HD = 256
MAGIC = 1536.0
CLIPV = 130.0
LN127 = float(np.log(127.0))
ENGS = ("sy", "ge", "ve", "se", "te")

oLNX = 0
oFT0 = 0
oFT1 = 1024
oEV0 = 2048
oEV1 = 3072
oT20 = 4096
oT21 = 5120
oWKV = 6144
oMIX = 8200
oMIXT = 16392
oKT = 17416
oRT = 25608
oVT2 = 33800
oRWK = 35848
oKH = 37896
oVH = 39944
oRH = 41992
oWP = 44040
oDEC = 47112
oRWF = 8200
oXQ = 0
oNATB = 24584
oWDQ = 17416
oPANG = 39944
oPANU = 41992
mSILO = 0
mUPS = 1024
mHCH = 2048
mPQG = 3072
mPQU = 5120
RB_ = (0, 5632)
HQ_ = (11264, 39944)
ARB_SIZE = 49920

fTMPA = 0
fZA = 0
fZB = 1025
fREC0 = 2050
fREC1 = 7170
fNUM0 = 3074
fNUM1 = 4098
fDEN0 = 5122
fDEN1 = 6146
fROWS = 3074
fROWV = 5123
fPTMP = 0
fROWG = 2816
fROWU = 5632
fROWD = 5632
fHTMP = 0
fOST = 2816
ARF_SIZE = 8448

hNMB = 0
hRSB = 2049
hRIB = 4098
hQ16 = 6147
hTMPL0 = 4098
hTMPL1 = 5123
hPT16 = 4098
hHT16 = 4098
ARH_SIZE = 8195

_CACHE = {}


def build_nc():
    nc = bass.Bass("TRN2", target_bir_lowering=False)

    xT = nc.dram_tensor("xT", [D, T + 1], F32, kind="ExternalInput").ap()
    wkT = nc.dram_tensor("wkT", [D, D], F32, kind="ExternalInput").ap()
    wvT = nc.dram_tensor("wvT", [D, D], F32, kind="ExternalInput").ap()
    wrT = nc.dram_tensor("wrT", [D, D], F32, kind="ExternalInput").ap()
    woT = nc.dram_tensor("woT", [D, D], F32, kind="ExternalInput").ap()
    wgT = nc.dram_tensor("wgT", [D, II], F32, kind="ExternalInput").ap()
    wuT = nc.dram_tensor("wuT", [D, II], F32, kind="ExternalInput").ap()
    wdT = nc.dram_tensor("wdT", [II, D], F32, kind="ExternalInput").ap()
    wgN = nc.dram_tensor("wgN", [II, D], F32, kind="ExternalInput").ap()
    wuN = nc.dram_tensor("wuN", [II, D], F32, kind="ExternalInput").ap()
    wdN = nc.dram_tensor("wdN", [D, II], F32, kind="ExternalInput").ap()
    tmki = nc.dram_tensor("tmki", [128, NT], F32, kind="ExternalInput").ap()
    tmvi = nc.dram_tensor("tmvi", [128, NT], F32, kind="ExternalInput").ap()
    tmri = nc.dram_tensor("tmri", [128, NT], F32, kind="ExternalInput").ap()
    tdec = nc.dram_tensor("tdec", [128, NT], F32, kind="ExternalInput").ap()
    tfir = nc.dram_tensor("tfir", [128, NT], F32, kind="ExternalInput").ap()
    clipx = nc.dram_tensor("clipx", [1, 1], F32, kind="ExternalInput").ap()
    clipd = nc.dram_tensor("clipd", [1, 1], F32, kind="ExternalInput").ap()
    pmask = nc.dram_tensor("pmask", [128, 8], F32, kind="ExternalInput").ap()
    outT = nc.dram_tensor("outT", [D, T], F32, kind="ExternalOutput").ap()

    h_sc = nc.dram_tensor("h_sc", [II, T], BF16, kind="Internal").ap()
    rw_sc = nc.dram_tensor("rw_sc", [D, T], BF16, kind="Internal").ap()
    cc_zin = nc.dram_tensor("cc_zin", [128, 16], F32, kind="Internal").ap()
    cc_zout = nc.dram_tensor("cc_zout", [1024, 16], F32, kind="Internal").ap()
    cc_m1i = nc.dram_tensor("cc_m1i", [1, 1], F32, kind="Internal").ap()
    cc_m1o = nc.dram_tensor("cc_m1o", [1, 1], F32, kind="Internal").ap()
    cc_m2i = nc.dram_tensor("cc_m2i", [1, 1], F32, kind="Internal").ap()
    cc_m2o = nc.dram_tensor("cc_m2o", [1, 1], F32, kind="Internal").ap()
    row_sc = nc.dram_tensor("row_sc", [1, II], F32, kind="Internal").ap()

    ctx = ExitStack()

    def sb(name, shape, dt):
        return ctx.enter_context(nc.sbuf_tensor(name, shape, dt))

    X = sb("X", [128, NT * (T + 1)], BF16)
    ARB = sb("ARB", [128, ARB_SIZE], BF16)
    MIXB = sb("MIXB", [128, NT * TH], BF16)
    ARF = sb("ARF", [128, ARF_SIZE], F32)
    ARH = sb("ARH", [128, ARH_SIZE], FP16)

    ZEA = sb("ZEA", [128, NT], F32)
    ZEB = sb("ZEB", [128, NT], F32)
    ZMSG = sb("ZMSG", [128, 16], F32)
    ZGATH = sb("ZGATH", [128, 128], F32)
    ZC = sb("ZC", [128, 16], F32)
    TMS = {"k": sb("tmk_s", [128, NT], F32), "v": sb("tmv_s", [128, NT], F32),
           "r": sb("tmr_s", [128, NT], F32)}
    OMS = {"k": sb("omk_s", [128, NT], F32), "v": sb("omv_s", [128, NT], F32),
           "r": sb("omr_s", [128, NT], F32)}
    TDS = sb("td_s", [128, NT], F32)
    UUS = sb("uu_s", [128, NT], F32)
    EUS = sb("eu_s", [128, NT], F32)
    WNEG = sb("wneg_s", [128, NT], F32)
    PHI = sb("phi_s", [128, NT], F32)
    DECF = sb("decf_s", [128, HD], F32)
    IOTA = sb("iota_s", [128, HD], F32)
    IOT1 = sb("iot1_s", [128, 128], F32)
    IOT2 = sb("iot2_s", [128, 128], F32)
    IDN = sb("idn_s", [128, 128], F32)
    ONES1B = sb("ones1b", [128, 1], BF16)
    ONES1F = sb("ones1f", [128, 1], F32)
    ONESR = sb("onesr", [1, 128], F32)
    CLXS = sb("clx_s", [1, 1], F32)
    CLDS = sb("cld_s", [1, 1], F32)
    PMS = sb("pm_s", [128, 8], F32)
    SC1 = sb("sc1_s", [1, 1], F32)
    SC2 = sb("sc2_s", [1, 1], F32)
    SC3 = sb("sc3_s", [1, 1], F32)
    SC4 = sb("sc4_s", [1, 1], F32)
    INVH = sb("invh_s", [128, 1], F32)
    SXPP = sb("sxpp_s", [128, 1], F32)
    SINVPP = sb("sinvpp_s", [128, 1], F32)
    MX8 = sb("mx8_s", [128, NT], F32)
    MXGU = sb("mxgu_s", [128, 4 * NI], F32)
    SHPP = sb("shpp_s", [128, 1], F32)
    MXACC = sb("mxacc_s", [128, 1], F32)
    MXCOL = sb("mxcol_s", [128, 1], F32)
    MAXR = sb("maxr_s", [1, 128], F32)
    SGRAW = sb("SGRAW", [128, NI], F32)
    SURAW = sb("SURAW", [128, NI], F32)
    SDRAW = sb("SDRAW", [128, NT], F32)
    SG = sb("SG", [128, NI], F32)
    SU = sb("SU", [128, NI], F32)
    SD = sb("SD", [128, NT], F32)
    TSG = sb("TSG", [32, 128], F32)
    EPS1 = sb("eps1_s", [1, 1], F32)
    L127 = sb("l127_s", [1, 1], F32)
    INVSB = sb("INVSB", [128, 2 * 128], BF16)

    PS = [ctx.enter_context(nc.psum_tensor(f"ps{j}", [128, TC], F32)) for j in range(3)]
    PST = ctx.enter_context(nc.psum_tensor("pst", [1, 3 * TC], F32))
    PBC = ctx.enter_context(nc.psum_tensor("pbc", [128, TC], F32))
    PTR = ctx.enter_context(nc.psum_tensor("ptr", [128, 256], F32))

    sem_names = ["sX", "sP0", "sP1", "sP2", "sHW", "sSR", "sWD", "sHS", "sHR0", "sHR1", "sN0", "sN1", "sN2", "sN3", "sRR0", "sRR1", "sRR2", "sRR3", "sRW0", "sRW1", "sRWR", "sSM", "sCC",
                 "sOUT", "cTE", "cVE", "cSE", "cGE",
                 "bar_sy", "bar_ge", "bar_ve", "bar_se", "bar_te"]
    sems = {n: ctx.enter_context(nc.semaphore(n)) for n in sem_names}

    def B(off, n):
        return ARB[:, off:off + n]

    def Fa(off, n):
        return ARF[:, off:off + n]

    def Ha(off, n):
        return ARH[:, off:off + n]

    def M(off, n):
        return MIXB[:, off:off + n]

    def Frow(off, n):
        return ARF[0:1, off:off + n]

    class _EngProxy:
        def __init__(self, e):
            self._e = e
            self.last = None

        def __getattr__(self, n):
            a = getattr(self._e, n)
            if callable(a):
                def w(*args, **kw):
                    r = a(*args, **kw)
                    if r is not None and hasattr(r, "then_inc"):
                        self.last = r
                    return r
                return w
            return a

    def emit(me, eng):
        eng = _EngProxy(eng)
        C = {n: 0 for n in sem_names}
        barn = [0]

        def on(name):
            return me == name

        def inc(key, n=1):
            C[key] += n
            return C[key]

        def tag(instr, key, n=1):
            if instr is not None:
                instr.then_inc(sems[key], n)
            inc(key, n)

        def wait(key, val=None):
            eng.wait_ge(sems[key], C[key] if val is None else val)

        def barrier():
            barn[0] += 1
            for n in ENGS:
                C["bar_" + n] += 1
            if me == "sy":
                eng.sem_inc(sems["bar_sy"], 1)
            else:
                eng.drain().then_inc(sems["bar_" + me], 1)
            for n in ENGS:
                if n != me:
                    eng.wait_ge(sems["bar_" + n], barn[0])

        if on("ge"):
            eng.dma_start(out=X[:], in_=xT.rearrange("(k p) t -> p k t", p=128)
                          ).then_inc(sems["sX"], 16)
        inc("sX", 16)
        for dst, src in ((TMS["k"], tmki), (TMS["v"], tmvi), (TMS["r"], tmri),
                         (TDS, tdec), (UUS, tfir), (CLXS, clipx), (CLDS, clipd),
                         (PMS, pmask)):
            if on("sy"):
                eng.dma_start(out=dst[:], in_=src).then_inc(sems["sSM"], 16)
            inc("sSM", 16)

        if on("ge"):
            eng.iota(IOTA[:], pattern=[[1, HD]], base=0, channel_multiplier=0,
                     allow_small_or_imprecise_dtypes=True)
            eng.iota(IOT1[:], pattern=[[1, 128]], base=0, channel_multiplier=0,
                     allow_small_or_imprecise_dtypes=True)
            eng.iota(IOT2[:], pattern=[[0, 128]], base=0, channel_multiplier=1,
                     allow_small_or_imprecise_dtypes=True).then_inc(sems["cGE"], 1)
        inc("cGE", 1)
        if on("ve"):
            eng.memset(EPS1[:], 1e-5)
            eng.memset(L127[:], LN127)
            eng.memset(ONES1B[:], 1.0)
            eng.memset(ONES1F[:], 1.0)
            eng.memset(ONESR[:], 1.0)
            eng.memset(MXACC[:], 0.0)
            wait("cGE")
            eng.tensor_tensor(out=IDN[:], in0=IOT1[:], in1=IOT2[:], op=ALU.is_equal)
            eng.drain().then_inc(sems["cVE"], 1)
        inc("cVE", 1)
        if on("se"):
            wait("sSM")
            eng.activation(out=WNEG[:], in_=TDS[:], func=AF.Exp)
            eng.activation(out=EUS[:], in_=UUS[:], func=AF.Exp)
            eng.drain().then_inc(sems["cSE"], 1)
        inc("cSE", 1)
        if on("ve"):
            wait("cSE")
            eng.tensor_scalar(out=WNEG[:], in0=WNEG[:], scalar1=-1.0, scalar2=None,
                              op0=ALU.mult)
            for p in ("k", "v", "r"):
                eng.tensor_scalar(out=OMS[p][:], in0=TMS[p][:], scalar1=-1.0,
                                  scalar2=1.0, op0=ALU.mult, op1=ALU.add)
            eng.drain().then_inc(sems["cVE"], 1)
        inc("cVE", 1)
        if on("se"):
            wait("cVE")
            eng.activation(out=PHI[:], in_=WNEG[:], func=AF.Exp)
            eng.drain().then_inc(sems["cSE"], 1)
        inc("cSE", 1)
        for i in range(NT):
            if on("ve"):
                wait("cSE")
                eng.tensor_scalar(out=DECF[:], in0=IOTA[:], scalar1=WNEG[:, i:i + 1],
                                  scalar2=None, op0=ALU.mult)
                tag(eng.last, "cVE")
            else:
                inc("cVE")
            if on("se"):
                wait("cVE")
                eng.activation(out=B(oDEC + i * HD, HD), in_=DECF[:], func=AF.Exp)
                tag(eng.last, "cSE")
            else:
                inc("cSE")
        barrier()

        def ln_stats(c0, W, keep=False):
            chunks = []
            o = 0
            while o < W:
                w = min(TC, W - o)
                chunks.append((o, w))
                o += w
            groups = [chunks[j:j + 3] for j in range(0, len(chunks), 3)]
            for gi1, grp in enumerate(groups):
                if on("te"):
                    wait("sX")
                    if gi1 > 0:
                        wait("cVE")
                    for ci, (o, w) in enumerate(grp):
                        for i in range(NT):
                            eng.matmul(PST[:, ci * TC:ci * TC + w], ONES1B[:],
                                       X[:, i * (T + 1) + c0 + o: i * (T + 1) + c0 + o + w],
                                       start=(i == 0), stop=(i == NT - 1))
                    tag(eng.last, "cTE")
                else:
                    inc("cTE")
                if on("ve"):
                    wait("cTE")
                    for ci, (o, w) in enumerate(grp):
                        eng.tensor_copy(out=Frow(fROWS + o, w), in_=PST[:, ci * TC:ci * TC + w])
                    tag(eng.last, "cVE")
                else:
                    inc("cVE")
            if on("ve"):
                eng.drain()
                eng.tensor_scalar(out=Frow(fROWS, W), in0=Frow(fROWS, W),
                                  scalar1=-1.0 / D, scalar2=None, op0=ALU.mult)
                tag(eng.last, "cVE")
            else:
                inc("cVE")
            bc_tags = []
            for ci, (o, w) in enumerate(chunks):
                bank = PBC if ci % 2 == 0 else PS[0]
                if on("te"):
                    if ci < 2:
                        wait("cVE")
                    else:
                        eng.wait_ge(sems["cVE"], bc_tags[ci - 2])
                    eng.matmul(bank[:, 0:w], ONESR[:], Frow(fROWS + o, w), start=True,
                               stop=True)
                    tag(eng.last, "cTE")
                else:
                    inc("cTE")
                if on("ve"):
                    wait("cTE")
                    eng.tensor_copy(out=Ha(hNMB + c0 + o, w), in_=bank[:, 0:w])
                    tag(eng.last, "cVE")
                else:
                    inc("cVE")
                bc_tags.append(C["cVE"])
            mm_tags = []
            sq_tags = []
            for i in range(NT):
                ad = oXQ + i * T if keep else oKT + (i % 2) * W
                sq = oKT + 2 * W + (i % 2) * W
                if on("ve"):
                    if i == 0:
                        wait("sX")
                        eng.drain()
                    if i >= 2 and not keep:
                        eng.wait_ge(sems["cSE"], sq_tags[i - 2])
                    eng.tensor_tensor(out=B(ad, W),
                                      in0=X[:, i * (T + 1) + c0: i * (T + 1) + c0 + W],
                                      in1=Ha(hNMB + c0, W), op=ALU.add)
                    tag(eng.last, "cVE")
                else:
                    inc("cVE")
                if on("se"):
                    wait("cVE")
                    if i >= 2:
                        eng.wait_ge(sems["cTE"], mm_tags[i - 2])
                    eng.activation(out=B(sq, W), in_=B(ad, W), func=AF.Square)
                    tag(eng.last, "cSE")
                else:
                    inc("cSE")
                sq_tags.append(C["cSE"])
                if on("te"):
                    wait("cSE")
                    if i == 0:
                        wait("cVE")
                    for ci, (o, w) in enumerate(chunks):
                        if ci < 3:
                            dst = PST[:, ci * TC:ci * TC + w]
                        elif ci == 3:
                            dst = PBC[0:1, 0:w]
                        else:
                            dst = PTR[0:1, 0:w]
                        eng.matmul(dst, ONES1B[:], B(sq + o, w),
                                   start=(i == 0), stop=(i == NT - 1))
                    tag(eng.last, "cTE")
                else:
                    inc("cTE")
                mm_tags.append(C["cTE"])
            if on("ve"):
                wait("cTE")
                for ci, (o, w) in enumerate(chunks):
                    if ci < 3:
                        src = PST[:, ci * TC:ci * TC + w]
                    elif ci == 3:
                        src = PBC[0:1, 0:w]
                    else:
                        src = PTR[0:1, 0:w]
                    eng.tensor_copy(out=Frow(fROWV + o, w), in_=src)
                tag(eng.last, "cVE")
            else:
                inc("cVE")
            if on("se"):
                wait("cVE")
                eng.activation(out=Frow(fROWV, W), in_=Frow(fROWV, W), func=AF.Ln,
                               bias=EPS1[:], scale=1.0 / D)
                eng.drain()
                eng.activation(out=Frow(fROWV, W), in_=Frow(fROWV, W), func=AF.Exp,
                               scale=-0.5)
                tag(eng.last, "cSE")
            else:
                inc("cSE")
            bc_tags = []
            for ci, (o, w) in enumerate(chunks):
                bank = PBC if ci % 2 == 0 else PS[0]
                if on("te"):
                    wait("cSE")
                    if ci < 2:
                        wait("cVE")
                    else:
                        eng.wait_ge(sems["cVE"], bc_tags[ci - 2])
                    eng.matmul(bank[:, 0:w], ONESR[:], Frow(fROWV + o, w), start=True,
                               stop=True)
                    tag(eng.last, "cTE")
                else:
                    inc("cTE")
                if on("ve"):
                    wait("cTE")
                    eng.tensor_copy(out=Ha(hRSB + c0 + o, w), in_=bank[:, 0:w])
                    tag(eng.last, "cVE")
                else:
                    inc("cVE")
                bc_tags.append(C["cVE"])
            barrier()

        ln_stats(0, T + 1)

        def nat_scales(src, ntile, width, dstcol):
            for tI in range(ntile):
                nb = oNATB + (tI % 2) * 5632
                sp = "sP%d" % (tI % 2)
                if on("ge"):
                    eng.wait_ge(sems["cVE"], max(0, C["cVE"] - 1))
                    eng.dma_start(out=B(nb, width),
                                  in_=src[tI * 128:(tI + 1) * 128, 0:width]
                                  ).then_inc(sems[sp], 16)
                inc(sp, 16)
                if on("ve"):
                    wait(sp)
                    eng.reduce_max(out=dstcol[:, tI:tI + 1], in_=B(nb, width),
                                   axis=AX.X, apply_absolute_value=True)
                    tag(eng.drain(), "cVE")
                else:
                    inc("cVE")

        def scales_to_row(scol, ncol, rowoff):
            if on("te"):
                wait("cVE")
                tag(eng.transpose(PTR[0:ncol, 0:128], scol[:], IDN[:]), "cTE")
            else:
                inc("cTE")
            if on("ve"):
                wait("cTE")
                wait("sSR")
                eng.tensor_copy(out=TSG[0:ncol, :], in_=PTR[0:ncol, 0:128])
                tag(eng.last, "cVE")
            else:
                inc("cVE")
            if on("ge"):
                wait("cVE")
                wait("sSR")
                eng.dma_start(out=row_sc[0:1, 0:ncol * 128], in_=TSG[0:ncol, :]
                              ).then_inc(sems["sSR"], 16)
            inc("sSR", 16)
            if on("ge"):
                wait("sSR")
                eng.wait_ge(sems["cVE"], C["cVE"])
                eng.dma_start(out=Frow(rowoff, ncol * 128),
                              in_=row_sc[0:1, 0:ncol * 128]).then_inc(sems["sSR"], 16)
            inc("sSR", 16)
            if on("se"):
                wait("sSR")
                eng.activation(out=Frow(rowoff, ncol * 128), in_=Frow(rowoff, ncol * 128),
                               func=AF.Ln)
                eng.drain()
                eng.activation(out=Frow(rowoff, ncol * 128), in_=Frow(rowoff, ncol * 128),
                               func=AF.Exp, scale=-1.0, bias=L127[:])
                tag(eng.last, "cSE")
            else:
                inc("cSE")


        def panel_src(srcT, e):
            return srcT.rearrange("(k p) ee -> p k ee", p=128)[:, :, e * 128:(e + 1) * 128]

        for h in range(2):
            c0 = h * TH
            half_tiles = []
            evq = []
            SLT = {}
            lnx_tags, add_tags = {}, {}
            ts_tags = {p: {} for p in ("k", "r", "v")}
            stt_tags = {p: {} for p in ("k", "r", "v")}
            mixt_reader = {}

            def panel_issue(srcw_, gt, el):
                slot = el % 3
                spn = "sP%d" % slot
                if on("ge"):
                    if slot in SLT:
                        eng.wait_ge(sems["cTE"], half_tiles[SLT[slot]])
                    eng.dma_start(out=B(oWP + slot * D, D),
                                  in_=panel_src(srcw_, el)).then_inc(sems[spn], 16)
                inc(spn, 16)
                SLT[slot] = gt

            for el in range(3):
                panel_issue(wkT, el, el)

            for i in range(NT):
                tl = hTMPL0 if i % 2 == 0 else hTMPL1
                if on("ve"):
                    if i >= 2:
                        sm2, v2 = lnx_tags[i - 2]
                        eng.wait_ge(sems[sm2], v2)
                    eng.tensor_tensor(out=Ha(tl, TH + 1),
                                      in0=X[:, i * (T + 1) + c0: i * (T + 1) + c0 + TH + 1],
                                      in1=Ha(hNMB + c0, TH + 1), op=ALU.add)
                    tag(eng.last, "cVE")
                else:
                    inc("cVE")
                add_tags[i] = C["cVE"]
                if i % 2 == 1:
                    if on("ge"):
                        eng.wait_ge(sems["cVE"], add_tags[i])
                        tag(eng.tensor_tensor(out=B(oLNX + i * (TH + 1), TH + 1),
                                              in0=Ha(tl, TH + 1),
                                              in1=Ha(hRSB + c0, TH + 1),
                                              op=ALU.mult), "cGE")
                    else:
                        inc("cGE")
                    lnx_tags[i] = ("cGE", C["cGE"])
                else:
                    if on("ve"):
                        eng.drain()
                        eng.tensor_tensor(out=B(oLNX + i * (TH + 1), TH + 1),
                                          in0=Ha(tl, TH + 1),
                                          in1=Ha(hRSB + c0, TH + 1), op=ALU.mult)
                        tag(eng.last, "cVE")
                    else:
                        inc("cVE")
                    lnx_tags[i] = ("cVE", C["cVE"])

            def mix_tile(p, i, dst):
                s_ = i % 2
                mslot = oMIXT if s_ == 0 else oDEC
                if on("se"):
                    lsem, lval = lnx_tags[i]
                    eng.wait_ge(sems[lsem], lval)
                    if s_ in mixt_reader:
                        eng.wait_ge(sems["cVE"], mixt_reader[s_])
                    eng.activation(out=B(mslot, TH), in_=B(oLNX + i * (TH + 1), TH),
                                   func=AF.Copy, scale=OMS[p][:, i:i + 1])
                    tag(eng.last, "cSE")
                else:
                    inc("cSE")
                ts_tags[p][i] = C["cSE"]
                if on("ve"):
                    eng.wait_ge(sems["cSE"], ts_tags[p][i])
                    if p == "v" and i == 0:
                        eng.wait_ge(sems["cTE"], half_tiles[7])
                    eng.scalar_tensor_tensor(out=dst[:, i * TH:(i + 1) * TH],
                                             in0=B(oLNX + i * (TH + 1) + 1, TH),
                                             scalar=TMS[p][:, i:i + 1],
                                             in1=B(mslot, TH),
                                             op0=ALU.mult, op1=ALU.add)
                    tag(eng.last, "cVE")
                else:
                    inc("cVE")
                stt_tags[p][i] = C["cVE"]
                mixt_reader[s_] = C["cVE"]

            PSXA = PS + [PBC]

            def proj_mms(p, srcw, dsto, gbase, mixap):
                for e in range(NT):
                    if p == "r" and e == 0:
                        panel_issue(srcw, gbase, 0)
                    if e + 1 < NT and (p == "r" or e >= 2):
                        panel_issue(srcw, gbase + e + 1, e + 1)
                    sp = "sP%d" % (e % 3)
                    for c in range(2):
                        ug = gbase * 2 + e * 2 + c
                        bank = PSXA[ug % 4]
                        if on("te"):
                            wait(sp)
                            if e == 0 and c == 0:
                                eng.wait_ge(sems["cVE"], stt_tags[p][7])
                            if ug >= 4:
                                eng.wait_ge(sems["cSE"], evq[ug - 4])
                            for ki in range(NT):
                                eng.matmul(bank[:],
                                           B(oWP + (e % 3) * D + ki * 128, 128),
                                           mixap[:, ki * TH + c * TC:
                                                 ki * TH + c * TC + TC],
                                           start=(ki == 0), stop=(ki == NT - 1))
                            tag(eng.last, "cTE")
                        else:
                            inc("cTE")
                        if on("se"):
                            wait("cTE")
                            o = dsto + e * TH + c * TC
                            if p == "r":
                                eng.activation(out=B(o, TC), in_=bank[:],
                                               func=AF.Sigmoid)
                            else:
                                eng.activation(out=B(o, TC), in_=bank[:],
                                               func=AF.Exp)
                            tag(eng.last, "cSE")
                        else:
                            inc("cSE")
                        evq.append(C["cSE"])
                    half_tiles.append(C["cTE"])

            for i in range(NT):
                mix_tile("k", i, ARB[:, oMIX:oMIX + NT * TH])
            for i in range(NT):
                mix_tile("r", i, MIXB)
            proj_mms("k", wkT, oKT, 0, ARB[:, oMIX:oMIX + NT * TH])
            for i in range(NT):
                mix_tile("v", i, ARB[:, oMIX:oMIX + NT * TH])
            proj_mms("r", wrT, oRT, 8, MIXB)

            te_chunk = {}
            evt2_tags, wk_tags, ft_tags, rc_tags, sc_tags = {}, {}, {}, {}, {}
            oFT_ = (oFT0, oFT1)
            oEV_ = (oEV0, oEV1)
            oT2_ = (oT20, oT21)
            fNUM_ = (fNUM0, fNUM1)
            fDEN_ = (fDEN0, fDEN1)
            fREC_ = (fREC0, fREC1)

            def scan_rc(e2):
                s2 = e2 % 2
                if on("se"):
                    eng.wait_ge(sems["cVE"], sc_tags[e2])
                    if e2 >= 2:
                        eng.wait_ge(sems["cVE"], wk_tags[e2 - 2])
                    eng.activation(out=Fa(fDEN_[s2], TH), in_=Fa(fDEN_[s2], TH),
                                   func=AF.Ln)
                    eng.drain()
                    eng.activation(out=Fa(fREC_[s2], TH), in_=Fa(fDEN_[s2], TH),
                                   func=AF.Exp, scale=-1.0)
                    tag(eng.last, "cSE")
                else:
                    inc("cSE")
                rc_tags[e2] = C["cSE"]

            def scan_wk(e2):
                s2 = e2 % 2
                g2 = h * NT + e2
                if on("ve"):
                    eng.wait_ge(sems["cSE"], rc_tags[e2])
                    if g2 >= 2:
                        eng.wait_ge(sems["sRW%d" % (e2 % 2)], 16 * ((g2 - 2) // 2 + 1))
                    eng.tensor_tensor(out=B(oWKV + s2 * TH, TH), in0=Fa(fNUM_[s2], TH),
                                      in1=Fa(fREC_[s2], TH), op=ALU.mult)
                    eng.drain()
                    eng.tensor_tensor(out=B(oRWK + s2 * TH, TH), in0=B(oWKV + s2 * TH, TH),
                                      in1=B(oRT + e2 * TH, TH), op=ALU.mult)
                    tag(eng.last, "cVE")
                else:
                    inc("cVE")
                wk_tags[e2] = C["cVE"]

            def scan_store(e2):
                rws = "sRW%d" % (e2 % 2)
                if on("sy"):
                    eng.wait_ge(sems["cVE"], wk_tags[e2])
                    eng.dma_start(out=rw_sc[e2 * 128:(e2 + 1) * 128,
                                            h * TH:(h + 1) * TH],
                                  in_=B(oRWK + (e2 % 2) * TH, TH)
                                  ).then_inc(sems[rws], 16)
                inc(rws, 16)

            for e in range(NT):
                s = e % 2
                vslot = oVT2 + s * TH
                ea = oKT + e * TH
                if e == 0:
                    panel_issue(wvT, 16, 0)
                if e + 1 < NT:
                    panel_issue(wvT, 16 + e + 1, e + 1)
                sp = "sP%d" % (e % 3)
                for c in range(2):
                    u = e * 2 + c
                    ug = 32 + u
                    bank = PSXA[ug % 4]
                    if on("te"):
                        wait(sp)
                        if e == 0 and c == 0:
                            eng.wait_ge(sems["cVE"], stt_tags["v"][7])
                        eng.wait_ge(sems["cSE"], evq[ug - 4])
                        for ki in range(NT):
                            eng.matmul(bank[:],
                                       B(oWP + (e % 3) * D + ki * 128, 128),
                                       B(oMIX + ki * TH + c * TC, TC),
                                       start=(ki == 0), stop=(ki == NT - 1))
                        tag(eng.last, "cTE")
                    else:
                        inc("cTE")
                    te_chunk[u] = C["cTE"]
                    if on("se"):
                        eng.wait_ge(sems["cTE"], te_chunk[u])
                        if e >= 2 and c == 0:
                            eng.wait_ge(sems["cGE"], evt2_tags[e - 2])
                            eng.wait_ge(sems["cVE"], sc_tags[e - 2])
                        eng.copy(out=B(vslot + c * TC, TC), in_=bank[:])
                        tag(eng.last, "cSE")
                    else:
                        inc("cSE")
                    evq.append(C["cSE"])
                half_tiles.append(C["cTE"])
                if on("se"):
                    if e == 0:
                        eng.wait_ge(sems["cVE"], stt_tags["v"][7])
                    eng.activation(out=B(oFT_[s], TH), in_=B(ea, TH), func=AF.Copy,
                                   scale=EUS[:, e:e + 1])
                    tag(eng.last, "cSE")
                else:
                    inc("cSE")
                ft_tags[e] = C["cSE"]
                if e >= 2:
                    scan_rc(e - 2)
                if on("ge"):
                    if e >= 2:
                        eng.wait_ge(sems["cVE"], sc_tags[e - 2])
                    eng.wait_ge(sems["cSE"], ft_tags[e])
                    eng.tensor_tensor(out=B(oEV_[s], TH), in0=B(ea, TH),
                                      in1=B(vslot, TH), op=ALU.mult)
                    tag(eng.tensor_tensor(out=B(oT2_[s], TH), in0=B(oFT_[s], TH),
                                          in1=B(vslot, TH), op=ALU.mult), "cGE")
                else:
                    inc("cGE")
                evt2_tags[e] = C["cGE"]
                if e >= 2:
                    scan_wk(e - 2)
                    scan_store(e - 2)
                if on("ve"):
                    eng.wait_ge(sems["cGE"], evt2_tags[e])
                    if e >= 2:
                        eng.wait_ge(sems["cSE"], rc_tags[e - 2])
                    else:
                        eng.wait_ge(sems["cSE"], ft_tags[e])
                    if e == 1:
                        eng.drain()
                    if h == 0:
                        eng.memset(Fa(fZA, 1), 0.0)
                        eng.memset(Fa(fZB, 1), 0.0)
                        ini_a, ini_b = 0.0, 0.0
                    else:
                        eng.tensor_copy(out=Fa(fZA, 1), in_=ZEA[:, e:e + 1])
                        eng.tensor_copy(out=Fa(fZB, 1), in_=ZEB[:, e:e + 1])
                        ini_a, ini_b = ZEA[:, e:e + 1], ZEB[:, e:e + 1]
                    eng.tensor_tensor_scan(out=Fa(fZA + 1, TH),
                                           data0=PHI[:, e:e + 1].broadcast_to([128, TH]),
                                           data1=B(oEV_[s], TH), initial=ini_a,
                                           op0=ALU.mult, op1=ALU.add)
                    eng.tensor_tensor_scan(out=Fa(fZB + 1, TH),
                                           data0=PHI[:, e:e + 1].broadcast_to([128, TH]),
                                           data1=B(ea, TH), initial=ini_b,
                                           op0=ALU.mult, op1=ALU.add)
                    eng.drain()
                    if h == 0:
                        eng.tensor_copy(out=ZEA[:, e:e + 1], in_=Fa(fZA + TH, 1))
                        eng.tensor_copy(out=ZEB[:, e:e + 1], in_=Fa(fZB + TH, 1))
                        eng.tensor_copy(out=B(oKH + e * HD, HD), in_=B(ea, HD))
                        eng.tensor_copy(out=B(oVH + e * HD, HD), in_=B(vslot, HD))
                        eng.tensor_copy(out=B(oRH + e * HD, HD), in_=B(oRT + e * TH, HD))
                    else:
                        eng.tensor_copy(out=ZMSG[:, e:e + 1], in_=Fa(fZA + TH, 1))
                        eng.tensor_copy(out=ZMSG[:, 8 + e:9 + e], in_=Fa(fZB + TH, 1))
                    eng.tensor_tensor(out=Fa(fNUM_[s], TH), in0=Fa(fZA, TH),
                                      in1=B(oT2_[s], TH), op=ALU.add)
                    eng.tensor_tensor(out=Fa(fDEN_[s], TH), in0=Fa(fZB, TH),
                                      in1=B(oFT_[s], TH), op=ALU.add)
                    tag(eng.last, "cVE")
                else:
                    inc("cVE")
                sc_tags[e] = C["cVE"]
            for e2 in (NT - 2, NT - 1):
                scan_rc(e2)
                scan_wk(e2)
            for e2 in (NT - 2, NT - 1):
                scan_store(e2)
            barrier()

        if on("ge"):
            eng.dma_start(out=cc_zin, in_=ZMSG[:]).then_inc(sems["sHW"], 16)
        inc("sHW", 16)
        if on("ge"):
            wait("sHW")
            eng.collective_compute("AllGather", ALU.bypass, ins=[cc_zin],
                                   outs=[cc_zout], replica_groups=[list(range(8))]
                                   ).then_inc(sems["sCC"], 1)
        inc("sCC", 1)
        for i in range(NT):
            if on("ve"):
                if i > 0:
                    wait("cSE", C["cSE"])
                eng.tensor_scalar(out=DECF[:], in0=IOTA[:], scalar1=WNEG[:, i:i + 1],
                                  scalar2=None, op0=ALU.mult)
                tag(eng.last, "cVE")
            else:
                inc("cVE")
            if on("se"):
                wait("cVE")
                eng.activation(out=B(oDEC + i * HD, HD), in_=DECF[:], func=AF.Exp)
                tag(eng.last, "cSE")
            else:
                inc("cSE")
        dec_se = C["cSE"]
        if on("sy"):
            wait("sRW0")
            wait("sRW1")
        for cr in range(4):
            if on("sy"):
                eng.dma_start(out=ARB[:, oRWF:oRWF + NT * T]
                              .rearrange("p (k t) -> p k t", t=T)
                              [:, :, cr * TC:(cr + 1) * TC],
                              in_=rw_sc.rearrange("(k p) t -> p k t", p=128)
                              [:, :, cr * TC:(cr + 1) * TC]
                              ).then_inc(sems["sRR%d" % cr], 16)
            inc("sRR%d" % cr, 16)
        PSX = PS + [PBC]
        nat_jobs = ([(wgN, tI, D, SGRAW) for tI in range(NI)]
                    + [(wuN, tI, D, SURAW) for tI in range(NI)]
                    + [(wdN, tI, II, SDRAW) for tI in range(NT)])
        nat_red_tags = []
        nat_pos = [0]

        def emit_nat(njobs):
            for _ in range(njobs):
                j = nat_pos[0]
                if j >= len(nat_jobs):
                    return
                nat_pos[0] += 1
                srcw_, tI_, width_, dstcol_ = nat_jobs[j]
                nb = oNATB + (j % 4) * 2816
                spn = "sN%d" % (j % 4)
                if on("ge"):
                    if j >= 4:
                        eng.wait_ge(sems["cVE"], nat_red_tags[j - 4])
                    eng.dma_start(out=B(nb, width_),
                                  in_=srcw_[tI_ * 128:(tI_ + 1) * 128, 0:width_]
                                  ).then_inc(sems[spn], 16)
                inc(spn, 16)
                if on("ve"):
                    wait(spn)
                    eng.reduce_max(out=dstcol_[:, tI_:tI_ + 1], in_=B(nb, width_),
                                   axis=AX.X, apply_absolute_value=True)
                    tag(eng.last, "cVE")
                else:
                    inc("cVE")
                nat_red_tags.append(C["cVE"])

        wo_te_tags = []
        wo_ev_tags = []
        wo_u = [0]

        def emit_wo_group(e, c):
            u = wo_u[0]
            wo_u[0] += 1
            pj = PSX[u % 4]
            sp = "sP%d" % (e % 3)
            if on("te"):
                wait("sRR%d" % c)
                wait(sp)
                if u >= 4:
                    eng.wait_ge(sems["cVE"], wo_ev_tags[u - 4])
                for ki in range(NT):
                    eng.matmul(pj[:], B(oWP + (e % 3) * D + ki * 128, 128),
                               B(oRWF + ki * T + c * TC, TC),
                               start=(ki == 0), stop=(ki == NT - 1))
                tag(eng.last, "cTE")
            else:
                inc("cTE")
            if on("ve"):
                wait("cTE")
                xo = e * (T + 1) + 1 + c * TC
                eng.tensor_tensor(out=X[:, xo:xo + TC], in0=X[:, xo:xo + TC],
                                  in1=pj[:], op=ALU.add)
                tag(eng.last, "cVE")
            else:
                inc("cVE")
            wo_ev_tags.append(C["cVE"])

        for e in range(NT):
            sp = "sP%d" % (e % 3)
            if on("ge"):
                if e >= 3:
                    eng.wait_ge(sems["cTE"], wo_te_tags[e - 3])
                eng.dma_start(out=B(oWP + (e % 3) * D, D),
                              in_=panel_src(woT, e)).then_inc(sems[sp], 16)
            inc(sp, 16)
            for c in (1, 2, 3):
                emit_wo_group(e, c)
            wo_te_tags.append(C["cTE"])
            emit_nat(7)
        emit_nat(len(nat_jobs))

        if on("ge"):
            wait("sCC")
            eng.dma_start(out=ZGATH[:],
                          in_=cc_zout.rearrange("(r p) f -> p r f", p=128)
                          ).then_inc(sems["sHW"], 16)
        inc("sHW", 16)
        if on("ve"):
            wait("sHW")
            eng.memset(ZC[:], 0.0)
            for r in range(8):
                eng.drain()
                eng.scalar_tensor_tensor(out=ZC[:], in0=ZGATH[:, r * 16:(r + 1) * 16],
                                         scalar=PMS[:, r:r + 1], in1=ZC[:],
                                         op0=ALU.mult, op1=ALU.add)
            tag(eng.last, "cVE")
        else:
            inc("cVE")

        zA8, zB8, fDENa, fRECa = 0, 2056, 4112, 6160
        bFT, bEV, bT2, bNUM = 0, 2048, 4096, 6144
        bWKV = 2048
        za_v = ARF[:, zA8:zA8 + 2056].rearrange("p (i t) -> p i t", t=257)[:, :, 0:256]
        zb_v = ARF[:, zB8:zB8 + 2056].rearrange("p (i t) -> p i t", t=257)[:, :, 0:256]
        t2_v = ARB[:, bT2:bT2 + 2048].rearrange("p (i t) -> p i t", t=256)
        ft_v = ARB[:, bFT:bFT + 2048].rearrange("p (i t) -> p i t", t=256)
        num_v = ARB[:, bNUM:bNUM + 2048].rearrange("p (i t) -> p i t", t=256)
        den_v = ARF[:, fDENa:fDENa + 2048].rearrange("p (i t) -> p i t", t=256)
        if on("se"):
            for i in range(NT):
                eng.activation(out=B(bFT + i * HD, HD), in_=B(oKH + i * HD, HD),
                               func=AF.Copy, scale=EUS[:, i:i + 1])
            tag(eng.last, "cSE")
        else:
            inc("cSE")
        if on("ve"):
            eng.tensor_tensor(out=B(bEV, 2048), in0=B(oKH, 2048), in1=B(oVH, 2048),
                              op=ALU.mult)
            for i in range(NT):
                eng.memset(Fa(zA8 + i * 257, 1), 0.0)
                eng.memset(Fa(zB8 + i * 257, 1), 0.0)
            eng.drain()
            for i in range(NT):
                eng.tensor_tensor_scan(out=Fa(zA8 + i * 257 + 1, HD),
                                       data0=PHI[:, i:i + 1].broadcast_to([128, HD]),
                                       data1=B(bEV + i * HD, HD), initial=0.0,
                                       op0=ALU.mult, op1=ALU.add)
                eng.tensor_tensor_scan(out=Fa(zB8 + i * 257 + 1, HD),
                                       data0=PHI[:, i:i + 1].broadcast_to([128, HD]),
                                       data1=B(oKH + i * HD, HD), initial=0.0,
                                       op0=ALU.mult, op1=ALU.add)
            eng.drain()
            eng.wait_ge(sems["cSE"], dec_se)
            for i in range(NT):
                eng.scalar_tensor_tensor(out=Fa(zA8 + i * 257, HD),
                                         in0=B(oDEC + i * HD, HD),
                                         scalar=ZC[:, i:i + 1],
                                         in1=Fa(zA8 + i * 257, HD),
                                         op0=ALU.mult, op1=ALU.add)
                eng.scalar_tensor_tensor(out=Fa(zB8 + i * 257, HD),
                                         in0=B(oDEC + i * HD, HD),
                                         scalar=ZC[:, 8 + i:9 + i],
                                         in1=Fa(zB8 + i * 257, HD),
                                         op0=ALU.mult, op1=ALU.add)
            wait("cSE")
            eng.tensor_tensor(out=B(bT2, 2048), in0=B(bFT, 2048), in1=B(oVH, 2048),
                              op=ALU.mult)
            eng.drain()
            eng.tensor_tensor(out=num_v, in0=za_v, in1=t2_v, op=ALU.add)
            eng.tensor_tensor(out=den_v, in0=zb_v, in1=ft_v, op=ALU.add)
            tag(eng.last, "cVE")
        else:
            inc("cVE")
        if on("se"):
            wait("cVE")
            eng.activation(out=Fa(fDENa, 2048), in_=Fa(fDENa, 2048), func=AF.Ln)
            eng.drain()
            eng.activation(out=Fa(fRECa, 2048), in_=Fa(fDENa, 2048), func=AF.Exp,
                           scale=-1.0)
            tag(eng.last, "cSE")
        else:
            inc("cSE")
        if on("ve"):
            wait("cSE")
            wait("sRR0")
            eng.tensor_tensor(out=B(bWKV, 2048), in0=B(bNUM, 2048),
                              in1=Fa(fRECa, 2048), op=ALU.mult)
            eng.drain()
            eng.tensor_tensor(out=ARB[:, oRWF:oRWF + NT * T]
                              .rearrange("p (i t) -> p i t", t=T)[:, :, 0:HD],
                              in0=ARB[:, bWKV:bWKV + 2048]
                              .rearrange("p (i t) -> p i t", t=256),
                              in1=ARB[:, oRH:oRH + 2048]
                              .rearrange("p (i t) -> p i t", t=256), op=ALU.mult)
            tag(eng.last, "cVE")
        else:
            inc("cVE")
        fixup_tag = C["cVE"]

        if on("te"):
            eng.wait_ge(sems["cVE"], fixup_tag)
        for e in range(NT):
            sp = "sP%d" % (e % 3)
            if on("ge"):
                lt = {0: 6, 1: 7, 2: 5}[e] if e < 3 else NT + e - 3
                eng.wait_ge(sems["cTE"], wo_te_tags[lt])
                eng.dma_start(out=B(oWP + (e % 3) * D, D),
                              in_=panel_src(woT, e)).then_inc(sems[sp], 16)
            inc(sp, 16)
            emit_wo_group(e, 0)
            wo_te_tags.append(C["cTE"])
        barrier()

        ln_stats(1, T, keep=True)

        for i in range(NT):
            if on("ve"):
                if i >= 1:
                    eng.drain()
                eng.tensor_tensor(out=Fa(fTMPA, T),
                                  in0=B(oXQ + i * T, T),
                                  in1=Ha(hRSB + 1, T), op=ALU.mult)
                eng.drain()
                eng.reduce_max(out=MX8[:, i:i + 1], in_=Fa(fTMPA, T), axis=AX.X,
                               apply_absolute_value=True)
                tag(eng.last, "cVE")
            else:
                inc("cVE")
        if on("ve"):
            eng.drain()
            eng.reduce_max(out=MXACC[:], in_=MX8[:], axis=AX.X,
                           apply_absolute_value=True)
            tag(eng.last, "cVE")
        else:
            inc("cVE")
        if on("te"):
            wait("cVE")
            tag(eng.transpose(PTR[0:1, 0:128], MXACC[:], IDN[:]), "cTE")
        else:
            inc("cTE")
        if on("ve"):
            wait("cTE")
            eng.tensor_copy(out=MAXR[:], in_=PTR[0:1, 0:128])
            eng.drain()
            eng.reduce_max(out=SC1[:], in_=MAXR[:], axis=AX.X, apply_absolute_value=True)
            tag(eng.last, "cVE")
        else:
            inc("cVE")
        if on("ge"):
            wait("cVE")
            eng.dma_start(out=cc_m1i, in_=SC1[:]).then_inc(sems["sHW"], 16)
        inc("sHW", 16)
        if on("ge"):
            wait("sHW")
            eng.collective_compute("AllReduce", ALU.max, ins=[cc_m1i], outs=[cc_m1o],
                                   replica_groups=[list(range(8))]
                                   ).then_inc(sems["sCC"], 1)
        inc("sCC", 1)

        scales_to_row(SDRAW, NT, fROWD)
        wdq_dst = ARB[:, oWDQ:oWDQ + NI * D].rearrange("p (k hh) -> p k hh", hh=D)
        wdq_g = {}
        for e in range(NT):
            if on("ge"):
                eng.wait_ge(sems["cVE"], C["cVE"])
                eng.dma_start(out=wdq_dst[:, :, e * 128:(e + 1) * 128],
                              in_=wdT.rearrange("(k p) hh -> p k hh", p=128)
                              [:, :, e * 128:(e + 1) * 128]).then_inc(sems["sWD"], 16)
            inc("sWD", 16)
            if on("te"):
                if e == 0:
                    wait("cSE")
                else:
                    wait("cVE")
                eng.matmul(PBC[:, 0:128], ONESR[:], Frow(fROWD + e * 128, 128),
                           start=True, stop=True)
                tag(eng.last, "cTE")
            else:
                inc("cTE")
            iv = (e % 2) * 128
            pt = fPTMP + (e % 2) * (NI * 128)
            hq = 4098
            if on("ve"):
                wait("cTE")
                if e >= 2:
                    eng.wait_ge(sems["cGE"], wdq_g[e - 2])
                eng.tensor_copy(out=INVSB[:, iv:iv + 128], in_=PBC[:, 0:128])
                tag(eng.last, "cVE")
            else:
                inc("cVE")
            wdq_iv = C["cVE"]
            if on("ge"):
                eng.wait_ge(sems["cVE"], wdq_iv)
                wait("sWD")
                tag(eng.tensor_tensor(
                    out=Fa(pt, NI * 128).rearrange("p (k f) -> p k f", f=128),
                    in0=wdq_dst[:, :, e * 128:(e + 1) * 128],
                    in1=INVSB[:, iv:iv + 128].rearrange("p (a f) -> p a f", a=1)
                    .broadcast_to([128, NI, 128]), op=ALU.mult), "cGE")
            else:
                inc("cGE")
            wdq_g[e] = C["cGE"]
            if on("ve"):
                wait("cGE")
                if e > 0:
                    eng.drain()
                eng.tensor_scalar(out=Ha(hq, NI * 128), in0=Fa(pt, NI * 128),
                                  scalar1=MAGIC, scalar2=None, op0=ALU.add)
                eng.drain()
                eng.tensor_scalar(out=wdq_dst[:, :, e * 128:(e + 1) * 128],
                                  in0=Ha(hq, NI * 128),
                                  scalar1=MAGIC, scalar2=None, op0=ALU.subtract)
                tag(eng.last, "cVE")
            else:
                inc("cVE")

        PSX = PS + [PBC]
        gu_load_tags = {}
        gu_bc_tags = {}
        gu_q_tags = {}
        gu_mm_tags = {}
        gu_se_tags = []
        gu_hv_tags = []
        gu_hg_tags = []

        def gu_load(tJ):
            spn = "sP%d" % (tJ % 2)
            pb = (tJ % 2) * 1024
            if on("ge"):
                if tJ >= 2:
                    eng.wait_ge(sems["cVE"], gu_q_tags[tJ - 2])
                eng.dma_start(out=B(oPANG + pb, 1024), in_=panel_src(wgT, tJ)
                              ).then_inc(sems[spn], 16)
                eng.dma_start(out=B(oPANU + pb, 1024), in_=panel_src(wuT, tJ)
                              ).then_inc(sems[spn], 16)
            inc(spn, 32)
            gu_load_tags[tJ] = (spn, C[spn])

        def gu_bcast(tJ):
            if on("te"):
                if tJ < 2:
                    wait("cSE")
                    wait("cVE")
                if tJ >= 1:
                    eng.wait_ge(sems["cVE"], gu_q_tags[tJ - 1])
                eng.matmul(PTR[:, 0:128], ONESR[:],
                           ARF[0:1, fROWG + tJ * 128: fROWG + tJ * 128 + 128],
                           start=True, stop=True)
                eng.matmul(PTR[:, 128:256], ONESR[:],
                           ARF[0:1, fROWU + tJ * 128: fROWU + tJ * 128 + 128],
                           start=True, stop=True)
                tag(eng.last, "cTE")
            else:
                inc("cTE")
            gu_bc_tags[tJ] = C["cTE"]

        def gu_quant(tJ):
            pb = (tJ % 2) * 1024
            spn, spv = gu_load_tags[tJ]
            if on("ve"):
                eng.wait_ge(sems["cTE"], gu_bc_tags[tJ])
                eng.wait_ge(sems[spn], spv)
                if tJ == 0:
                    eng.wait_ge(sems["cGE"], C["cGE"])
                if tJ >= 2:
                    eng.wait_ge(sems["cTE"], gu_mm_tags[tJ - 2])
                eng.tensor_copy(out=INVSB[:, 0:256], in_=PTR[:, 0:256])
                for which, po, qo in ((0, oPANG, mPQG), (1, oPANU, mPQU)):
                    eng.drain()
                    eng.tensor_tensor(out=Fa(fPTMP, 1024).rearrange("p (k f) -> p k f", f=128),
                                      in0=B(po + pb, 1024).rearrange("p (k f) -> p k f", f=128),
                                      in1=INVSB[:, which * 128:(which + 1) * 128]
                                      .rearrange("p (a f) -> p a f", a=1)
                                      .broadcast_to([128, NT, 128]), op=ALU.mult)
                    eng.drain()
                    eng.tensor_scalar(out=Ha(hPT16, 1024), in0=Fa(fPTMP, 1024),
                                      scalar1=MAGIC, scalar2=None, op0=ALU.add)
                    eng.drain()
                    eng.tensor_scalar(out=M(qo + pb, 1024), in0=Ha(hPT16, 1024),
                                      scalar1=MAGIC, scalar2=None, op0=ALU.subtract)
                tag(eng.last, "cVE")
            else:
                inc("cVE")
            gu_q_tags[tJ] = C["cVE"]

        scales_to_row(SGRAW, NI, fROWG)
        scales_to_row(SURAW, NI, fROWU)
        gu_load(0)
        gu_bcast(0)
        gu_quant(0)
        gu_load(1)
        gu_bcast(1)
        gu_quant(1)

        if on("ge"):
            wait("sCC")
            eng.dma_start(out=SC2[:], in_=cc_m1o).then_inc(sems["sHW"], 16)
        inc("sHW", 16)
        if on("se"):
            wait("sHW")
            eng.activation(out=SC3[:], in_=CLXS[:], func=AF.Sigmoid)
            tag(eng.last, "cSE")
        else:
            inc("cSE")
        if on("ve"):
            wait("cSE")
            eng.tensor_tensor(out=SC3[:], in0=SC3[:], in1=SC2[:], op=ALU.mult)
            eng.drain()
            eng.tensor_scalar(out=SC3[:], in0=SC3[:], scalar1=1.0 / 127.0, scalar2=None,
                              op0=ALU.mult)
            tag(eng.last, "cVE")
        else:
            inc("cVE")
        if on("se"):
            wait("cVE")
            eng.activation(out=SC4[:], in_=SC3[:], func=AF.Ln)
            eng.drain()
            eng.activation(out=SC4[:], in_=SC4[:], func=AF.Exp, scale=-1.0)
            tag(eng.last, "cSE")
        else:
            inc("cSE")
        if on("te"):
            wait("cSE")
            wait("cVE")
            eng.matmul(PBC[:, 0:1], ONESR[:], SC3[:], start=True, stop=True)
            eng.matmul(PBC[:, 1:2], ONESR[:], SC4[:], start=True, stop=True)
            tag(eng.last, "cTE")
        else:
            inc("cTE")
        if on("ve"):
            wait("cTE")
            eng.tensor_copy(out=SXPP[:], in_=PBC[:, 0:1])
            eng.tensor_copy(out=SINVPP[:], in_=PBC[:, 1:2])
            eng.drain()
            eng.tensor_scalar(out=SG[:], in0=SGRAW[:], scalar1=SXPP[:],
                              scalar2=1.0 / 127.0, op0=ALU.mult, op1=ALU.mult)
            eng.tensor_scalar(out=SU[:], in0=SURAW[:], scalar1=SXPP[:],
                              scalar2=1.0 / 127.0, op0=ALU.mult, op1=ALU.mult)
            tag(eng.last, "cVE")
        else:
            inc("cVE")
        barrier()
        xq_col = {}
        for c4 in range(4):
            co = c4 * TC
            if on("ve"):
                for g4 in range(2):
                    eng.drain()
                    for i in range(g4 * 4, g4 * 4 + 4):
                        eng.tensor_tensor(out=Fa(fTMPA + (i % 4) * TC, TC),
                                          in0=B(oXQ + i * T + co, TC),
                                          in1=Ha(hRSB + 1 + co, TC), op=ALU.mult)
                    eng.drain()
                    for i in range(g4 * 4, g4 * 4 + 4):
                        eng.tensor_scalar(out=Ha(hQ16 + (i % 4) * TC, TC),
                                          in0=Fa(fTMPA + (i % 4) * TC, TC),
                                          scalar1=SINVPP[:], scalar2=MAGIC,
                                          op0=ALU.mult, op1=ALU.add)
                    eng.drain()
                    for i in range(g4 * 4, g4 * 4 + 4):
                        eng.tensor_scalar(out=Ha(hQ16 + (i % 4) * TC, TC),
                                          in0=Ha(hQ16 + (i % 4) * TC, TC),
                                          scalar1=MAGIC + 127.0, scalar2=MAGIC - 128.0,
                                          op0=ALU.min, op1=ALU.max)
                    eng.drain()
                    for i in range(g4 * 4, g4 * 4 + 4):
                        eng.tensor_scalar(out=B(oXQ + i * T + co, TC),
                                          in0=Ha(hQ16 + (i % 4) * TC, TC),
                                          scalar1=MAGIC, scalar2=None, op0=ALU.subtract)
                tag(eng.last, "cVE")
            else:
                inc("cVE")
            xq_col[c4] = C["cVE"]
        for tI in range(NI):
            pb = (tI % 2) * 1024
            if 2 <= tI + 1 < NI:
                gu_load(tI + 1)
                gu_bcast(tI + 1)
                gu_quant(tI + 1)
            for c in range(4):
                m = tI * 4 + c
                bg = PSX[(2 * m) % 4]
                bu = PSX[(2 * m + 1) % 4]
                if on("te"):
                    if tI == 0:
                        eng.wait_ge(sems["cVE"], xq_col[c])
                    if c == 0:
                        eng.wait_ge(sems["cVE"], gu_q_tags[tI])
                    if m >= 2:
                        eng.wait_ge(sems["cSE"], gu_se_tags[m - 2])
                    for ki in range(NT):
                        eng.matmul(bg[:], M(mPQG + pb + ki * 128, 128),
                                   B(oXQ + ki * T + c * TC, TC),
                                   start=(ki == 0), stop=(ki == NT - 1))
                    tag(eng.last, "cTE")
                    for ki in range(NT):
                        eng.matmul(bu[:], M(mPQU + pb + ki * 128, 128),
                                   B(oXQ + ki * T + c * TC, TC),
                                   start=(ki == 0), stop=(ki == NT - 1))
                    tag(eng.last, "cTE")
                else:
                    inc("cTE", 2)
                gu_mm_tags[tI] = C["cTE"]
                so = mSILO + (m % 2) * TC
                uo = mUPS + (m % 2) * TC
                if on("se"):
                    eng.wait_ge(sems["cTE"], C["cTE"] - 1)
                    if m >= 2:
                        eng.wait_ge(sems["cGE"], gu_hg_tags[m - 2])
                    eng.activation(out=M(so, TC), in_=bg[:], func=AF.Silu,
                                   scale=SG[:, tI:tI + 1])
                    wait("cTE")
                    eng.activation(out=M(uo, TC), in_=bu[:], func=AF.Copy,
                                   scale=SU[:, tI:tI + 1])
                    tag(eng.last, "cSE")
                else:
                    inc("cSE")
                gu_se_tags.append(C["cSE"])
                hb = mHCH + (m % 2) * TC
                if on("ge"):
                    wait("cSE")
                    if m >= 2:
                        wait("sHS")
                        eng.wait_ge(sems["cVE"], gu_hv_tags[m - 2])
                    tag(eng.tensor_tensor(out=M(hb, TC), in0=M(so, TC), in1=M(uo, TC),
                                          op=ALU.mult), "cGE")
                else:
                    inc("cGE")
                gu_hg_tags.append(C["cGE"])
                if on("ve"):
                    wait("cGE")
                    eng.reduce_max(out=MXGU[:, m:m + 1], in_=M(hb, TC), axis=AX.X,
                                   apply_absolute_value=True)
                    tag(eng.last, "cVE")
                else:
                    inc("cVE")
                gu_hv_tags.append(C["cVE"])
                if on("sy"):
                    wait("cGE")
                    eng.dma_start(out=h_sc[tI * 128:(tI + 1) * 128, c * TC:(c + 1) * TC],
                                  in_=M(hb, TC)).then_inc(sems["sHS"], 16)
                inc("sHS", 16)
        barrier()

        if on("ve"):
            eng.reduce_max(out=MXACC[:], in_=MXGU[:], axis=AX.X,
                           apply_absolute_value=True)
            tag(eng.last, "cVE")
        else:
            inc("cVE")
        if on("te"):
            wait("cVE")
            tag(eng.transpose(PTR[0:1, 0:128], MXACC[:], IDN[:]), "cTE")
        else:
            inc("cTE")
        if on("ve"):
            wait("cTE")
            eng.tensor_copy(out=MAXR[:], in_=PTR[0:1, 0:128])
            eng.drain()
            eng.reduce_max(out=SC1[:], in_=MAXR[:], axis=AX.X, apply_absolute_value=True)
            tag(eng.last, "cVE")
        else:
            inc("cVE")
        if on("ge"):
            wait("cVE")
            eng.dma_start(out=cc_m2i, in_=SC1[:]).then_inc(sems["sHW"], 16)
        inc("sHW", 16)
        if on("ge"):
            wait("sHW")
            eng.collective_compute("AllReduce", ALU.max, ins=[cc_m2i], outs=[cc_m2o],
                                   replica_groups=[list(range(8))]
                                   ).then_inc(sems["sCC"], 1)
        inc("sCC", 1)
        if on("ge"):
            wait("sCC")
            eng.dma_start(out=SC2[:], in_=cc_m2o).then_inc(sems["sHW"], 16)
        inc("sHW", 16)
        if on("se"):
            wait("sHW")
            eng.activation(out=SC3[:], in_=CLDS[:], func=AF.Sigmoid)
            tag(eng.last, "cSE")
        else:
            inc("cSE")
        if on("ve"):
            wait("cSE")
            eng.tensor_tensor(out=SC3[:], in0=SC3[:], in1=SC2[:], op=ALU.mult)
            eng.drain()
            eng.tensor_scalar(out=SC3[:], in0=SC3[:], scalar1=1.0 / 127.0, scalar2=None,
                              op0=ALU.mult)
            tag(eng.last, "cVE")
        else:
            inc("cVE")
        if on("se"):
            wait("cVE")
            eng.activation(out=SC4[:], in_=SC3[:], func=AF.Ln)
            eng.drain()
            eng.activation(out=SC4[:], in_=SC4[:], func=AF.Exp, scale=-1.0)
            tag(eng.last, "cSE")
        else:
            inc("cSE")
        if on("te"):
            wait("cSE")
            eng.matmul(PBC[:, 0:1], ONESR[:], SC3[:], start=True, stop=True)
            eng.matmul(PBC[:, 1:2], ONESR[:], SC4[:], start=True, stop=True)
            tag(eng.last, "cTE")
        else:
            inc("cTE")
        if on("ve"):
            wait("cTE")
            eng.tensor_copy(out=SHPP[:], in_=PBC[:, 0:1])
            eng.tensor_copy(out=INVH[:], in_=PBC[:, 1:2])
            eng.drain()
            eng.tensor_scalar(out=SD[:], in0=SDRAW[:], scalar1=SHPP[:],
                              scalar2=1.0 / 127.0, op0=ALU.mult, op1=ALU.mult)
            tag(eng.last, "cVE")
        else:
            inc("cVE")
        barrier()

        TC2 = 256
        NC2 = T // TC2
        HRW = NI * TC2
        dn_q_tags = {}
        dn_mm_tags = {}
        dn_ev_tags = []
        dn_u = [0]

        def dn_read(cJ):
            sl = RB_[cJ % 2]
            spn = "sHR%d" % (cJ % 2)
            if on("sy"):
                if cJ == 0:
                    wait("sHS")
                if cJ >= 2:
                    eng.wait_ge(sems["cVE"], dn_q_tags[cJ - 2])
                half1 = 11
                eng.dma_start(out=B(sl, half1 * TC2),
                              in_=h_sc.rearrange("(k p) (cc t) -> p k cc t", p=128, cc=NC2)
                              [:, 0:half1, cJ, :]).then_inc(sems[spn], 16)
                eng.dma_start(out=B(sl + half1 * TC2, (NI - half1) * TC2),
                              in_=h_sc.rearrange("(k p) (cc t) -> p k cc t", p=128, cc=NC2)
                              [:, half1:NI, cJ, :]).then_inc(sems[spn], 16)
            inc(spn, 32)

        def dn_quant(cJ):
            sl = RB_[cJ % 2]
            ql = HQ_[cJ % 2]
            spn = "sHR%d" % (cJ % 2)
            if on("ve"):
                wait(spn)
                if cJ >= 2:
                    eng.wait_ge(sems["cTE"], dn_mm_tags[cJ - 2])
                for hh in range(2):
                    qo = hh * 2816
                    eng.drain()
                    eng.tensor_scalar(out=Ha(hHT16, 2816), in0=B(sl + qo, 2816),
                                      scalar1=INVH[:], scalar2=MAGIC, op0=ALU.mult,
                                      op1=ALU.add)
                    eng.drain()
                    eng.tensor_scalar(out=Ha(hHT16, 2816), in0=Ha(hHT16, 2816),
                                      scalar1=MAGIC + 127.0, scalar2=MAGIC - 128.0,
                                      op0=ALU.min, op1=ALU.max)
                    eng.drain()
                    eng.tensor_scalar(out=B(ql + qo, 2816), in0=Ha(hHT16, 2816),
                                      scalar1=MAGIC, scalar2=None, op0=ALU.subtract)
                tag(eng.last, "cVE")
            else:
                inc("cVE")
            dn_q_tags[cJ] = C["cVE"]

        dn_read(0)
        dn_quant(0)
        for c in range(NC2):
            ql = HQ_[c % 2]
            if c + 1 < NC2:
                dn_read(c + 1)
            for e in range(NT):
                if e == 4 and c + 1 < NC2:
                    dn_quant(c + 1)
                u = dn_u[0]
                dn_u[0] += 1
                pj = PSX[u % 4]
                if on("te"):
                    if e == 0:
                        eng.wait_ge(sems["cVE"], dn_q_tags[c])
                    if u >= 4:
                        eng.wait_ge(sems["cVE"], dn_ev_tags[u - 4])
                    for ki in range(NI):
                        eng.matmul(pj[:, 0:TC2],
                                   ARB[:, oWDQ + ki * D + e * 128: oWDQ + ki * D + (e + 1) * 128],
                                   B(ql + ki * TC2, TC2),
                                   start=(ki == 0), stop=(ki == NI - 1))
                    tag(eng.last, "cTE")
                else:
                    inc("cTE")
                dn_mm_tags[c] = C["cTE"]
                ob = fOST + (u % 2) * TC
                if on("ve"):
                    wait("cTE")
                    if u >= 2:
                        wait("sOUT")
                    xo = e * (T + 1) + 1 + c * TC2
                    eng.scalar_tensor_tensor(out=Fa(ob, TC2), in0=pj[:, 0:TC2],
                                             scalar=SD[:, e:e + 1], in1=X[:, xo:xo + TC2],
                                             op0=ALU.mult, op1=ALU.add)
                    tag(eng.last, "cVE")
                else:
                    inc("cVE")
                dn_ev_tags.append(C["cVE"])
                if on("sy"):
                    wait("cVE")
                    eng.dma_start(out=outT[e * 128:(e + 1) * 128, c * TC2:(c + 1) * TC2],
                                  in_=Fa(ob, TC2)).then_inc(sems["sOUT"], 16)
                inc("sOUT", 16)
        if on("sy"):
            wait("sOUT")
        barrier()
        return C

    with nc.Block() as block:
        @block.sync
        def _(e):
            emit("sy", e)

        @block.gpsimd
        def _(e):
            emit("ge", e)

        @block.vector
        def _(e):
            emit("ve", e)

        @block.scalar
        def _(e):
            emit("se", e)

        @block.tensor
        def _(e):
            emit("te", e)

    ctx.close()
    return nc


def kernel(L0_attn_x, L0_attn_time_mix_k, L0_attn_time_mix_v, L0_attn_time_mix_r,
           L0_attn_time_decay, L0_attn_time_first, L0_attn_Wk, L0_attn_Wv,
           L0_attn_Wr, L0_attn_Wo, L0_ffn_clip_x, L0_ffn_clip_down,
           L0_ffn_w_gate, L0_ffn_w_up, L0_ffn_w_down):
    x = np.ascontiguousarray(np.asarray(L0_attn_x, dtype=np.float32))
    f32 = lambda a: np.ascontiguousarray(np.asarray(a, dtype=np.float32))
    tT = lambda a: np.ascontiguousarray(np.asarray(a, dtype=np.float32).T)

    if "nc" not in _CACHE:
        _CACHE["nc"] = build_nc()
    nc = _CACHE["nc"]

    wkT_, wvT_, wrT_, woT_ = tT(L0_attn_Wk), tT(L0_attn_Wv), tT(L0_attn_Wr), tT(L0_attn_Wo)
    wgT_, wuT_, wdT_ = tT(L0_ffn_w_gate), tT(L0_ffn_w_up), tT(L0_ffn_w_down)
    wgN_, wuN_, wdN_ = f32(L0_ffn_w_gate), f32(L0_ffn_w_up), f32(L0_ffn_w_down)

    def chan(a):
        return np.ascontiguousarray(np.asarray(a, dtype=np.float32).reshape(NT, 128).T)

    tmk_, tmv_, tmr_ = (chan(L0_attn_time_mix_k), chan(L0_attn_time_mix_v),
                        chan(L0_attn_time_mix_r))
    td_, tf_ = chan(L0_attn_time_decay), chan(L0_attn_time_first)
    clx_ = f32(L0_ffn_clip_x).reshape(1, 1)
    cld_ = f32(L0_ffn_clip_down).reshape(1, 1)

    in_maps = []
    for c in range(8):
        b, q = divmod(c, 4)
        xs = np.zeros((D, T + 1), np.float32)
        s0 = q * T
        xs[:, 1:] = x[b, s0:s0 + T].T
        if q > 0:
            xs[:, 0] = x[b, s0 - 1]
        pm = np.zeros((128, 8), np.float32)
        if q > 0:
            pm[:, c - 1] = 1.0
        in_maps.append({
            "xT": np.ascontiguousarray(xs), "wkT": wkT_, "wvT": wvT_, "wrT": wrT_,
            "woT": woT_, "wgT": wgT_, "wuT": wuT_, "wdT": wdT_,
            "wgN": wgN_, "wuN": wuN_, "wdN": wdN_,
            "tmki": tmk_, "tmvi": tmv_, "tmri": tmr_, "tdec": td_, "tfir": tf_,
            "clipx": clx_, "clipd": cld_, "pmask": pm,
        })

    res = run_bass_kernel_spmd(nc, in_maps, core_ids=list(range(8)))
    out = np.empty((2, 8192, D), np.float32)
    for c in range(8):
        b, q = divmod(c, 4)
        out[b, q * T:(q + 1) * T] = res.results[c]["outT"].T
    return out
